# version 14
# baseline (speedup 1.0000x reference)
import contextlib
import numpy as np
import ml_dtypes

import concourse.bass as bass
import concourse.mybir as mybir
from concourse.bass_utils import run_bass_kernel_spmd
import concourse.bass_utils as _bu

def _enable_ldw_opt():
    f = _bu.bir_verify_and_optimise
    if any(c == "--enable-ldw-opt=false" for c in f.__code__.co_consts
           if isinstance(c, str)):
        f.__code__ = f.__code__.replace(co_consts=tuple(
            "--enable-ldw-opt=true" if c == "--enable-ldw-opt=false" else c
            for c in f.__code__.co_consts))


F32 = mybir.dt.float32
F16 = mybir.dt.float16
BF16 = mybir.dt.bfloat16
I32 = mybir.dt.int32
ALU = mybir.AluOpType
AF = mybir.ActivationFunctionType
AX = mybir.AxisListType

SCALE2 = float(np.float32(np.sqrt(1.0 - 0.95)) * np.float32(np.sqrt(1.0 - 0.95)))
DECAY = 0.95
EPS = 1e-5


class Ctx:

    def __init__(self, me, eng, sems):
        self.me = me
        self.e = eng
        self.sems = sems
        self.c = {}
        self.g = {}

    def do(self, name, f, inc=None, n=1):
        inst = f(self.e[name]) if name == self.me else None
        if inc is not None:
            self.c[inc] = self.c.get(inc, 0) + n
            if inst is not None:
                inst.then_inc(self.sems[inc], n)

    def wait(self, name, sem, val):
        if name == self.me and val > 0:
            self.e[name].wait_ge(self.sems[sem], val)

    def val(self, sem):
        return self.c.get(sem, 0)

    def barrier(self):
        for s in list(self.sems):
            v = self.c.get(s, 0)
            if v > 0:
                self.e[self.me].wait_ge(self.sems[s], v)


def build_nc(S, N, debug=False):
    E = 2048
    M = 2048
    KT = E // 128
    MT = M // 128
    MH = MT // 2
    TT = S // 128
    TC = S // 512
    NCH = N // 256
    TH = S // 2
    THW = min(512, TH)
    THC = TH // THW
    OCT = min(1024, S)
    NOC = S // OCT
    OTT = OCT // 128
    NK = N // 128

    nc = bass.Bass("TRN2", target_bir_lowering=False)

    x_in = nc.dram_tensor("x", [S, E], F32, kind="ExternalInput").ap()
    wg = nc.dram_tensor("wg", [E, M], F32, kind="ExternalInput").ap()
    wv = nc.dram_tensor("wv", [E, M], F32, kind="ExternalInput").ap()
    wq = nc.dram_tensor("wq", [E, M], F32, kind="ExternalInput").ap()
    wo = nc.dram_tensor("wo", [M, E], F32, kind="ExternalInput").ap()
    bg_in = nc.dram_tensor("bg", [M], F32, kind="ExternalInput").ap()
    bq_in = nc.dram_tensor("bq", [M], F32, kind="ExternalInput").ap()
    gpkT = nc.dram_tensor("gpkT", [E // 2, N], I32, kind="ExternalInput").ap()
    upkT = nc.dram_tensor("upkT", [E // 2, N], I32, kind="ExternalInput").ap()
    gscT = nc.dram_tensor("gscT", [E // 32, N], F32, kind="ExternalInput").ap()
    uscT = nc.dram_tensor("uscT", [E // 32, N], F32, kind="ExternalInput").ap()
    dpkT = nc.dram_tensor("dpkT", [N // 2, E], I32, kind="ExternalInput").ap()
    dscT = nc.dram_tensor("dscT", [N // 32, E], F32, kind="ExternalInput").ap()
    identin = nc.dram_tensor("identin", [128, 128], BF16, kind="ExternalInput").ap()
    out = nc.dram_tensor("out", [S, E], F32, kind="ExternalOutput").ap()

    kk = "ExternalOutput" if debug else "Internal"
    xnT_h = nc.dram_tensor("xnT_h", [E, S], BF16, kind=kk).ap()
    load_h = nc.dram_tensor("load_h", [M, S], BF16, kind=kk).ap()
    s_h = nc.dram_tensor("s_h", [S, E], BF16, kind=kk).ap()
    inter_h = nc.dram_tensor("inter_h", [N, S], BF16, kind=kk).ap()
    wgu_h = nc.dram_tensor("wgu_h", [NCH, 128, 2 * KT * 256], BF16).ap()
    wd_h = nc.dram_tensor("wd_h", [N, E], BF16, kind=kk).ap()

    with contextlib.ExitStack() as st:
        sems = {}
        slot_names = (
            ["pre", "w0", "wg1", "wv1", "wq1", "wo0", "wo1", "p3x", "T", "V", "A"]
            + [f"p0x{i}" for i in range(2)] + [f"p0st{i}" for i in range(2)]
            + [f"a1x{i}" for i in range(2)] + [f"a2l{i}" for i in range(2)]
            + [f"a3l{i}" for i in range(2)] + [f"a3s{i}" for i in range(2)]
            + [f"p3p{i}" for i in range(2)] + [f"p3s{i}" for i in range(2)]
            + [f"p3r{i}" for i in range(2)] + [f"p3i{i}" for i in range(4)]
            + [f"p4p{i}" for i in range(2)] + [f"p4w{i}" for i in range(2)]
            + [f"p5i{i}" for i in range(6)] + [f"p5w{i}" for i in range(6)]
            + [f"p5x{i}" for i in range(16)] + [f"p5o{i}" for i in range(4)]
        )
        for s_name in slot_names:
            sems[s_name] = st.enter_context(nc.semaphore(s_name))

        ident = st.enter_context(nc.sbuf_tensor("ident", [128, 128], BF16))
        bgt = st.enter_context(nc.sbuf_tensor("bgt", [128, MT], F32))
        bqt = st.enter_context(nc.sbuf_tensor("bqt", [128, MT], F32))
        dec = st.enter_context(nc.sbuf_tensor("dec", [128, 512], F32))

        PS = nc.alloc_psum_tensor("PS", [128, 4096], F32)
        arena0 = (nc._sbuf_addr_for_side("left") + 63) & ~63
        arena_end = 229344
        assert arena_end - arena0 >= 209900, f"arena too small: {arena_end - arena0}"

        sizes = {F32: 4, BF16: 2, F16: 2, I32: 4}

        def place(name, cols, dtype, off):
            sz = sizes[dtype] * cols
            assert off % 64 == 0, name
            assert arena0 + off + sz <= arena_end, f"{name}: overflow {off + sz}"
            return nc.alloc_sbuf_tensor_at(name, [128, cols], dtype,
                                           offset=arena0 + off)

        p0_x = [place(f"p0x{i}", E, F32, 8192 * i) for i in range(2)]
        p0_sq = place("p0sq", E, F32, 16384)
        p0_xn = place("p0xn", E, BF16, 24576)
        p0_st = [place(f"p0st{i}", 512, BF16, 28672 + 1024 * i) for i in range(2)]
        p0_s1 = place("p0s1", 16, F32, 30720)[:, 0:1]
        p0_s2 = place("p0s2", 16, F32, 30784)[:, 0:1]
        p0_m = place("p0m", 16, F32, 30848)[:, 0:1]
        p0_v = place("p0v", 16, F32, 30912)[:, 0:1]
        p0_r = place("p0r", 16, F32, 30976)[:, 0:1]
        p0_nm = place("p0nm", 16, F32, 31040)[:, 0:1]
        p0_mm = place("p0mm", 16, F32, 31104)[:, 0:1]

        awG = place("awG", KT * 1024, BF16, 33536)
        awV = place("awV", KT * 1024, BF16, 66304)
        awQ = place("awQ", KT * 1024, BF16, 99072)
        awG2 = place("awG2", KT * 1024, BF16, 131840)

        a12_xc = [place(f"a12xc{i}", KT * 512, BF16, 164608 + 16384 * i)
                  for i in range(2)]
        a12_st = place("a12st", 512, F32, 197376)
        a12_gt = [place(f"a12gt{i}", 512, F32, 199424 + 2048 * i) for i in range(2)]
        a12_qt = [place(f"a12qt{i}", 512, F32, 203520 + 2048 * i) for i in range(2)]
        a12_mt = place("a12mt", 512, F32, 207616)
        a12_ld = [place(f"a12ld{i}", 512, BF16, 31232 + 1024 * i) for i in range(2)]
        a12_cr = place("a12cr", MT, F32, 33280)

        wo_a = place("wo_a", MH * E, BF16, 33536)
        wo_b = place("wo_b", MH * E, BF16, 131840)
        a3_lc = [place(f"a3lc{i}", MT * 512, BF16, 164608 + 16384 * i)
                 for i in range(2)]
        a3_ss = [place(f"a3ss{i}", E, BF16, 197376 + 4096 * i) for i in range(2)]

        p3_xh = place("p3xh", KT * TH, BF16, 66304)
        p3_w = [place(f"p3w{i}", 2 * KT * 256, BF16, 16384 * i) for i in range(2)]
        p3_pk = [place("p3pk0", 2 * 8 * 256, I32, 32768),
                 place("p3pk1", 2 * 8 * 256, I32, 131840)]
        p3_t1 = place("p3t1", 256, I32, 49152)
        p3_t2 = place("p3t2", 256, I32, 50176)
        p3_t3 = place("p3t3", 256, I32, 51200)
        p3_t4 = place("p3t4", 256, I32, 52224)
        p3_lo = place("p3lo", 256, I32, 53248)
        p3_hi = place("p3hi", 256, I32, 54272)
        p3_sc = [place(f"p3sc{i}", 2 * 8 * 256, F32, 148224 + 16384 * i)
                 for i in range(2)]
        p3_gl = [place(f"p3gl{i}", THC * THW, BF16, 55296 + 4096 * i)
                 for i in range(2)]
        p3_is = [place(f"p3is{i}", TH, BF16, 180992 + sizes[BF16] * TH * i)
                 for i in range(2)]
        p4_pk = [p3_pk[0], p3_pk[1]]
        p4_sc = [p3_sc[0], p3_sc[1]]
        p4_w = [place(f"p4w{i}", E, BF16,
                      180992 + sizes[BF16] * TH * 2 + 4096 * i) for i in range(4)]

        p5_ib = [place(f"p5ib{i}", OCT, BF16, sizes[BF16] * OCT * i)
                 for i in range(6)]
        p5_wb = [place(f"p5wb{i}", 512, BF16, 12288 + 1024 * i) for i in range(6)]
        p5_g = [place(f"p5g{i}", 512, F32, 18432 + 2048 * i) for i in range(8)]
        p5_xs = [place(f"p5xs{i}", 512, F32, 34816 + 2048 * i) for i in range(16)]
        p5_ssl = [place(f"p5ssl{i}", 512, F32, 67584 + 2048 * i) for i in range(16)]
        p5_o = [place(f"p5o{i}", 512, F32, 100352 + 2048 * i) for i in range(4)]

        def load_w_half(p: Ctx, dst, srcW, mh, sem):
            for h in (0, 1):
                p.do("gpsimd", lambda e, h=h, dst=dst, srcW=srcW, mh=mh: e.dma_start(
                    out=dst[:].rearrange("p (kt c) -> p kt c", c=1024)[:, h::2, :],
                    in_=srcW[h:E:2, 1024 * mh:1024 * (mh + 1)]
                    .rearrange("(i p) c -> p i c", p=128)), sem, 16)
            return p.val(sem)

        def load_wo_half(p: Ctx, dst, a, sem):
            p.do("gpsimd", lambda e, dst=dst, a=a: e.dma_start(
                out=dst[:].rearrange("p (i c) -> p i c", c=E),
                in_=wo[1024 * a:1024 * (a + 1), :]
                .rearrange("(i p) c -> p i c", p=128)), sem, 16)
            return p.val(sem)

        def dequant_chain(p: Ctx, src, C):
            p.do("vector", lambda e: e.tensor_scalar(
                p3_t1[:, 0:C], src, 7, 9, ALU.bitwise_and, ALU.logical_shift_left), "V")
            p.do("vector", lambda e: e.tensor_scalar(
                p3_t2[:, 0:C], src, 8, 12, ALU.bitwise_and, ALU.logical_shift_left), "V")
            p.do("vector", lambda e: e.tensor_tensor(
                p3_lo[:, 0:C], p3_t1[:, 0:C], p3_t2[:, 0:C], ALU.bitwise_or), "V")
            p.do("vector", lambda e: e.tensor_scalar(
                p3_t3[:, 0:C], src, 0x70, 5, ALU.bitwise_and, ALU.logical_shift_left), "V")
            p.do("vector", lambda e: e.tensor_scalar(
                p3_t4[:, 0:C], src, 0x80, 8, ALU.bitwise_and, ALU.logical_shift_left), "V")
            p.do("vector", lambda e: e.tensor_tensor(
                p3_hi[:, 0:C], p3_t3[:, 0:C], p3_t4[:, 0:C], ALU.bitwise_or), "V")

        def ln_tile(p: Ctx, m, i, predone):
            sl = i % 2
            xs = f"p0x{sl}"
            p.wait("sync", "V", m.get(("xn", i - 2), 0))
            p.wait("sync", "A", m.get(("sq", i - 2), 0))
            p.do("sync", lambda e, i=i, sl=sl: e.dma_start(
                out=p0_x[sl][:], in_=x_in[128 * i:128 * (i + 1), :]), xs, 16)
            p.wait("scalar", xs, p.val(xs))
            p.wait("scalar", "V", m.get(("var", i - 1), 0))
            p.do("scalar", lambda e, sl=sl: e.activation(
                p0_sq[:], p0_x[sl][:], AF.Square), "A", 1)
            m[("sq", i)] = p.val("A")
            p.wait("vector", xs, p.val(xs))
            p.wait("vector", "A", m[("sq", i)])
            p.do("vector", lambda e: e.reduce_sum(
                p0_s2, p0_sq[:], axis=AX.X), "V")
            p.do("vector", lambda e, sl=sl: e.reduce_sum(
                p0_s1, p0_x[sl][:], axis=AX.X), "V", 1)
            p.wait("vector", "V", p.val("V"))
            p.do("vector", lambda e: e.tensor_scalar(
                p0_m, p0_s1, 1.0 / E, None, ALU.mult), "V")
            p.do("vector", lambda e: e.tensor_scalar(
                p0_v, p0_s2, 1.0 / E, None, ALU.mult), "V", 1)
            p.wait("vector", "V", p.val("V"))
            p.do("vector", lambda e: e.scalar_tensor_tensor(
                p0_mm, p0_m, -1.0, p0_m, ALU.mult, ALU.mult), "V", 1)
            p.wait("vector", "V", p.val("V"))
            p.do("vector", lambda e: e.tensor_tensor(
                p0_v, p0_v, p0_mm, ALU.add), "V")
            p.do("vector", lambda e: e.tensor_scalar(
                p0_v, p0_v, EPS, None, ALU.add), "V", 1)
            m[("var", i)] = p.val("V")
            p.wait("scalar", "V", m[("var", i)])
            p.do("scalar", lambda e: e.activation(p0_r, p0_v, AF.Ln), "A")
            p.do("scalar", lambda e: e.activation(
                p0_r, p0_r, AF.Exp, scale=-0.5), "A", 1)
            m[("rstd", i)] = p.val("A")
            p.wait("vector", "A", m[("rstd", i)])
            p.do("vector", lambda e: e.scalar_tensor_tensor(
                p0_nm, p0_m, -1.0, p0_r, ALU.mult, ALU.mult), "V", 1)
            p.wait("vector", "V", p.val("V"))
            p.do("vector", lambda e, sl=sl: e.tensor_scalar(
                p0_xn[:], p0_x[sl][:], p0_r, p0_nm,
                ALU.mult, ALU.add), "V", 1)
            m[("xn", i)] = p.val("V")

            for g in range(4):
                gi = 4 * i + g
                pslot = 6 + gi % 2
                sts = f"p0st{gi % 2}"
                p.wait("tensor", "pre", predone)
                p.wait("tensor", "V", m[("xn", i)])
                p.wait("tensor", "V", m.get(("ev", gi - 2), 0))
                for j in range(4):
                    kt = 4 * g + j
                    ib, h = kt // 2, kt % 2
                    p.do("tensor", lambda e, ib=ib, h=h, pslot=pslot, j=j: e.matmul(
                        PS[:, 512 * pslot + 128 * j:512 * pslot + 128 * (j + 1)],
                        p0_xn[:, 256 * ib + h:256 * (ib + 1):2],
                        ident[:], start=True, stop=True),
                        "T", 1 if j == 3 else 0)
                m[("tp", gi)] = p.val("T")
                p.wait("vector", "T", m[("tp", gi)])
                p.wait("vector", sts, p.val(sts))
                p.do("vector", lambda e, pslot=pslot, gi=gi: e.tensor_copy(
                    p0_st[gi % 2][:], PS[:, 512 * pslot:512 * (pslot + 1)]),
                    "V", 1)
                m[("ev", gi)] = p.val("V")
                p.wait("sync", "V", m[("ev", gi)])
                p.do("sync", lambda e, g=g, i=i, gi=gi: e.dma_start(
                    out=xnT_h.rearrange("(kt p) c -> p kt c", p=128)
                    [:, 4 * g:4 * (g + 1), 128 * i:128 * (i + 1)],
                    in_=p0_st[gi % 2][:].rearrange("p (j c) -> p j c", c=128)),
                    sts, 16)
            m[("stw0", i)] = p.val("p0st0")
            m[("stw1", i)] = p.val("p0st1")

        def a12_tc(p: Ctx, m, mh, tc, G, wgsem, wgdone, wvsem, wvdone,
                   wqsem, wqdone):
            sl = tc % 2
            xs = f"a1x{sl}"
            p.wait("sync", "T", m.get(("mmq", (tc - 2, MH - 1)), 0))
            p.wait("sync", "p0st0", m.get(("stw0", 4 * tc + 3), 0))
            p.wait("sync", "p0st1", m.get(("stw1", 4 * tc + 3), 0))
            p.do("sync", lambda e, tc=tc, sl=sl: e.dma_start(
                out=a12_xc[sl][:].rearrange("p (kt c) -> p kt c", c=512),
                in_=xnT_h.rearrange("(kt p) c -> p kt c", p=128)
                [:, :, 512 * tc:512 * (tc + 1)]), xs, 16)
            m[("xc", tc)] = p.val(xs)
            for mi8 in range(MH):
                mi = MH * mh + mi8
                it = tc * MH + mi8
                pg, pv, pq = (it % 2) * 3, (it % 2) * 3 + 1, (it % 2) * 3 + 2
                ls = f"a2l{it % 2}"
                p.wait("tensor", xs, m[("xc", tc)])
                p.wait("tensor", wgsem, wgdone)
                p.wait("tensor", "A", m.get(("sigg", it - 2), 0))
                for kt in range(KT):
                    p.do("tensor", lambda e, kt=kt, mi8=mi8, sl=sl, pg=pg, G=G: e.matmul(
                        PS[:, 512 * pg:512 * (pg + 1)],
                        G[:, 1024 * kt + 128 * mi8:1024 * kt + 128 * (mi8 + 1)],
                        a12_xc[sl][:, 512 * kt:512 * (kt + 1)],
                        start=(kt == 0), stop=(kt == KT - 1)),
                        "T", 1 if kt == KT - 1 else 0)
                m[("mmg", it)] = p.val("T")
                p.wait("tensor", wvsem, wvdone)
                p.wait("tensor", "V", m.get(("st", it - 2), 0))
                for kt in range(KT):
                    p.do("tensor", lambda e, kt=kt, mi8=mi8, sl=sl, pv=pv: e.matmul(
                        PS[:, 512 * pv:512 * (pv + 1)],
                        awV[:, 1024 * kt + 128 * mi8:1024 * kt + 128 * (mi8 + 1)],
                        a12_xc[sl][:, 512 * kt:512 * (kt + 1)],
                        start=(kt == 0), stop=(kt == KT - 1)),
                        "T", 1 if kt == KT - 1 else 0)
                m[("mmv", it)] = p.val("T")
                p.wait("tensor", wqsem, wqdone)
                p.wait("tensor", "A", m.get(("sigq", it - 2), 0))
                for kt in range(KT):
                    p.do("tensor", lambda e, kt=kt, mi8=mi8, sl=sl, pq=pq: e.matmul(
                        PS[:, 512 * pq:512 * (pq + 1)],
                        awQ[:, 1024 * kt + 128 * mi8:1024 * kt + 128 * (mi8 + 1)],
                        a12_xc[sl][:, 512 * kt:512 * (kt + 1)],
                        start=(kt == 0), stop=(kt == KT - 1)),
                        "T", 1 if kt == KT - 1 else 0)
                m[("mmq", (tc, mi8))] = p.val("T")
                p.wait("scalar", "T", m[("mmg", it)])
                p.wait("scalar", "V", m.get(("st", it - 2), 0))
                p.do("scalar", lambda e, mi=mi, it=it, pg=pg: e.activation(
                    a12_gt[it % 2][:], PS[:, 512 * pg:512 * (pg + 1)],
                    AF.Sigmoid, bias=bgt[:, mi:mi + 1]), "A", 1)
                m[("sigg", it)] = p.val("A")
                p.wait("scalar", "T", m[("mmq", (tc, mi8))])
                p.wait("scalar", "V", m.get(("ld", it - 2), 0))
                p.do("scalar", lambda e, mi=mi, it=it, pq=pq: e.activation(
                    a12_qt[it % 2][:], PS[:, 512 * pq:512 * (pq + 1)],
                    AF.Sigmoid, bias=bqt[:, mi:mi + 1]), "A", 1)
                m[("sigq", it)] = p.val("A")
                p.wait("vector", "T", m[("mmv", it)])
                p.wait("vector", "A", m[("sigg", it)])
                p.do("vector", lambda e, it=it, pv=pv: e.tensor_tensor(
                    a12_st[:], a12_gt[it % 2][:],
                    PS[:, 512 * pv:512 * (pv + 1)], ALU.mult), "V", 1)
                m[("st", it)] = p.val("V")
                init = 0.0 if tc == 0 else a12_cr[:, mi8:mi8 + 1]
                p.do("vector", lambda e, init=init: e.tensor_tensor_scan(
                    a12_mt[:], dec[:], a12_st[:],
                    init, ALU.mult, ALU.add), "V", 1)
                p.wait("vector", "V", p.val("V"))
                p.do("vector", lambda e, mi8=mi8: e.tensor_copy(
                    a12_cr[:, mi8:mi8 + 1], a12_mt[:, 511:512]), "V", 1)
                p.wait("vector", "A", m[("sigq", it)])
                p.wait("vector", ls, p.val(ls))
                p.do("vector", lambda e, it=it: e.scalar_tensor_tensor(
                    a12_ld[it % 2][:], a12_mt[:], SCALE2,
                    a12_qt[it % 2][:], ALU.mult, ALU.mult), "V", 1)
                m[("ld", it)] = p.val("V")
                p.wait("scalar", "V", m[("ld", it)])
                p.do("scalar", lambda e, mi=mi, tc=tc, it=it: e.dma_start(
                    out=load_h[128 * mi:128 * (mi + 1), 512 * tc:512 * (tc + 1)],
                    in_=a12_ld[it % 2][:]), ls, 16)

        def phase_p0_a12mh0(p: Ctx):
            m = {}
            load_w_half(p, awG, wg, 0, "w0")
            load_w_half(p, awV, wv, 0, "w0")
            p.g["w0"] = load_w_half(p, awQ, wq, 0, "w0")
            p.g["wg1"] = load_w_half(p, awG2, wg, 1, "wg1")
            p.do("sync", lambda e: e.dma_start(out=ident[:], in_=identin), "pre", 16)
            for mi in range(MT):
                p.do("sync", lambda e, mi=mi: e.dma_start(
                    out=bgt[:, mi:mi + 1],
                    in_=bg_in[128 * mi:128 * (mi + 1)].unsqueeze(1)), "pre", 16)
                p.do("sync", lambda e, mi=mi: e.dma_start(
                    out=bqt[:, mi:mi + 1],
                    in_=bq_in[128 * mi:128 * (mi + 1)].unsqueeze(1)), "pre", 16)
            predone = p.val("pre")
            p.do("vector", lambda e: e.memset(dec[:], DECAY), "V")
            w0 = p.g["w0"]
            done = 0
            for i in range(TT):
                ln_tile(p, m, i, predone)
                if i >= 7 and (i - 7) % 4 == 0:
                    a12_tc(p, m, 0, (i - 7) // 4, awG,
                           "w0", w0, "w0", w0, "w0", w0)
                    done = (i - 7) // 4 + 1
            for tc in range(done, TC):
                a12_tc(p, m, 0, tc, awG, "w0", w0, "w0", w0, "w0", w0)
            p.barrier()

        def phase_a12mh1(p: Ctx):
            m = {}
            G = awG2
            wgdone = p.g["wg1"]
            wvdone = load_w_half(p, awV, wv, 1, "wv1")
            wqdone = load_w_half(p, awQ, wq, 1, "wq1")
            p.g["wo0"] = load_wo_half(p, wo_a, 0, "wo0")
            for tc in range(TC):
                m[("stw0", 4 * tc + 3)] = 0
                m[("stw1", 4 * tc + 3)] = 0
                a12_tc(p, m, 1, tc, G, "wg1", wgdone, "wv1", wvdone,
                       "wq1", wqdone)
            p.barrier()

        def phase_a3(p: Ctx):
            m = {}
            woadone = p.g["wo0"]
            wobdone = load_wo_half(p, wo_b, 1, "wo1")
            for kt in range(KT):
                p.do("scalar", lambda e, kt=kt: e.dma_start(
                    out=p3_xh[:, TH * kt:TH * (kt + 1)],
                    in_=xnT_h[128 * kt:128 * (kt + 1), 0:TH]), "p3x", 16)
            p.g["p3x0"] = p.val("p3x")
            for tc in range(TC):
                sl = tc % 2
                lsname = f"a3l{sl}"
                p.wait("sync", "T", m.get(("mm", (tc - 2, 3)), 0))
                p.do("sync", lambda e, tc=tc, sl=sl: e.dma_start(
                    out=a3_lc[sl][:].rearrange("p (mi c) -> p mi c", c=512),
                    in_=load_h.rearrange("(mi p) c -> p mi c", p=128)
                    [:, :, 512 * tc:512 * (tc + 1)]), lsname, 16)
                m[("lc", tc)] = p.val(lsname)
                for ts in range(4):
                    it = tc * 4 + ts
                    pb = (it % 2) * 4
                    ssn = f"a3s{it % 2}"
                    p.wait("tensor", lsname, m[("lc", tc)])
                    p.wait("tensor", "wo0", woadone)
                    p.wait("tensor", "wo1", wobdone)
                    p.wait("tensor", "V", m.get(("ev", it - 2), 0))
                    for mi in range(MT):
                        for ec in range(4):
                            wob = wo_a if mi < MH else wo_b
                            p.do("tensor", lambda e, mi=mi, ec=ec, ts=ts, sl=sl, pb=pb, wob=wob: e.matmul(
                                PS[:, 512 * (pb + ec):512 * (pb + ec + 1)],
                                a3_lc[sl][:, 512 * mi + 128 * ts:512 * mi + 128 * (ts + 1)],
                                wob[:, E * (mi % MH) + 512 * ec:E * (mi % MH) + 512 * (ec + 1)],
                                start=(mi == 0), stop=(mi == MT - 1)),
                                "T", 1 if (mi == MT - 1 and ec == 3) else 0)
                    m[("mm", (tc, ts))] = p.val("T")
                    p.wait("vector", "T", m[("mm", (tc, ts))])
                    p.wait("vector", ssn, p.val(ssn))
                    p.do("vector", lambda e, it=it, pb=pb: e.tensor_copy(
                        a3_ss[it % 2][:], PS[:, 512 * pb:512 * (pb + 4)]), "V", 1)
                    m[("ev", it)] = p.val("V")
                    p.wait("scalar", "V", m[("ev", it)])
                    p.do("scalar", lambda e, tc=tc, ts=ts, it=it: e.dma_start(
                        out=s_h[512 * tc + 128 * ts:512 * tc + 128 * (ts + 1), :],
                        in_=a3_ss[it % 2][:]), ssn, 16)
            p.barrier()

        def phase_p3(p: Ctx, half):
            m = {}
            if half == 0:
                xdone = p.g["p3x0"]
            else:
                for kt in range(KT):
                    p.do("sync", lambda e, kt=kt: e.dma_start(
                        out=p3_xh[:, TH * kt:TH * (kt + 1)],
                        in_=xnT_h[128 * kt:128 * (kt + 1), TH:2 * TH]), "p3x", 16)
                xdone = p.val("p3x")

            def dequant_pieces(nch):
                wsl = nch % 2
                pks = f"p3p{wsl}"
                sps = f"p3s{wsl}"
                p.wait("sync", "V", m.get(("deq", nch - 2), 0))
                for m_i, pk in ((0, gpkT), (1, upkT)):
                    p.do("sync", lambda e, m_i=m_i, pk=pk, nch=nch, wsl=wsl: e.dma_start(
                        out=p3_pk[wsl][:, 2048 * m_i:2048 * (m_i + 1)]
                        .rearrange("p (kb c) -> p kb c", c=256),
                        in_=pk.rearrange("(kb p) c -> p kb c", p=128)
                        [:, :, 256 * nch:256 * (nch + 1)]), pks, 16)
                for m_i, sc in ((0, gscT), (1, uscT)):
                    for kb in range(8):
                        p.do("sync", lambda e, m_i=m_i, sc=sc, kb=kb, nch=nch, wsl=wsl: e.dma_start(
                            out=p3_sc[wsl][:, 2048 * m_i + 256 * kb:2048 * m_i + 256 * (kb + 1)],
                            in_=sc[8 * kb:8 * (kb + 1), 256 * nch:256 * (nch + 1)]
                            .unsqueeze(1).broadcast_to([8, 16, 256])), pks, 16)
                m[("pk", nch)] = p.val(pks)

                pieces = []

                def first_piece():
                    p.wait("vector", pks, m[("pk", nch)])
                    p.wait("vector", "T", m.get(("mmall", nch - 2), 0))
                    p.wait("vector", sps, p.val(sps))
                pieces.append(first_piece)

                def mk_piece(m_i, kb, last):
                    def piece():
                        wsl_ = nch % 2
                        coff = 2048 * m_i + 256 * kb
                        dequant_chain(p, p3_pk[wsl_][:, coff:coff + 256], 256)
                        for src_, kt2 in ((p3_lo, 2 * kb), (p3_hi, 2 * kb + 1)):
                            p.do("vector", lambda e, src_=src_, kt2=kt2, m_i=m_i, coff=coff, wsl_=wsl_: e.scalar_tensor_tensor(
                                p3_w[wsl_][:, 4096 * m_i + 256 * kt2:4096 * m_i + 256 * (kt2 + 1)],
                                src_[:].bitcast(F16)[:, 0:512:2], 16384.0,
                                p3_sc[wsl_][:, coff:coff + 256],
                                ALU.mult, ALU.mult), "V",
                                1 if last else 0)
                        if last:
                            m[("deq", nch)] = p.val("V")
                            m[("wready", nch)] = ("V", m[("deq", nch)])
                            p.wait("scalar", "V", m[("deq", nch)])
                            p.do("scalar", lambda e, nch=nch, wsl_=wsl_: e.dma_start(
                                out=wgu_h[nch], in_=p3_w[wsl_][:]),
                                f"p3s{wsl_}", 16)
                    return piece
                for m_i in range(2):
                    for kb in range(8):
                        pieces.append(mk_piece(m_i, kb, m_i == 1 and kb == 7))
                return pieces

            def wload_part(nch):
                wsl = nch % 2
                rls = f"p3r{wsl}"
                p.wait("sync", "T", m.get(("mmall", nch - 2), 0))
                p.do("sync", lambda e, nch=nch, wsl=wsl: e.dma_start(
                    out=p3_w[wsl][:], in_=wgu_h[nch]), rls, 16)
                m[("wready", nch)] = (rls, p.val(rls))

            def p4_pieces(j):
                sl = j % 2
                pks = f"p4p{sl}"
                wws = f"p4w{sl}"
                p.wait("gpsimd", "V", m.get(("p4deq", j - 2), 0))
                p.do("gpsimd", lambda e, j=j, sl=sl: e.dma_start(
                    out=p4_pk[sl][:, 0:E], in_=dpkT[128 * j:128 * (j + 1), :]), pks, 16)
                p.do("gpsimd", lambda e, j=j, sl=sl: e.dma_start(
                    out=p4_sc[sl][:, 0:E], in_=dscT[8 * j:8 * (j + 1), :]
                    .unsqueeze(1).broadcast_to([8, 16, E])), pks, 16)
                pass
                m[("p4pk", j)] = p.val(pks)

                pieces = []

                def first_piece():
                    p.wait("vector", pks, m[("p4pk", j)])
                    p.wait("vector", wws, p.val(wws))
                pieces.append(first_piece)

                def mk_piece(c, last):
                    def piece():
                        co = 256 * c
                        dequant_chain(p, p4_pk[sl][:, co:co + 256], 256)
                        for src_, wi in ((p3_lo, 0), (p3_hi, 1)):
                            p.do("vector", lambda e, src_=src_, wi=wi, sl=sl, co=co: e.scalar_tensor_tensor(
                                p4_w[2 * sl + wi][:, co:co + 256],
                                src_[:].bitcast(F16)[:, 0:512:2], 16384.0,
                                p4_sc[sl][:, co:co + 256],
                                ALU.mult, ALU.mult), "V",
                                1 if last else 0)
                        if last:
                            m[("p4deq", j)] = p.val("V")
                            p.wait("scalar", "V", m[("p4deq", j)])
                            for wi in range(2):
                                pt = 2 * j + wi
                                p.do("scalar", lambda e, pt=pt, sl=sl, wi=wi: e.dma_start(
                                    out=wd_h[128 * pt:128 * (pt + 1), :],
                                    in_=p4_w[2 * sl + wi][:]), wws, 16)
                    return piece
                for c in range(E // 256):
                    pieces.append(mk_piece(c, c == E // 256 - 1))
                return pieces

            def mm_part(nch, pieces=None):
                pieces = list(pieces) if pieces else []
                npieces = len(pieces)
                niter = 3
                popped = 0
                wsl = nch % 2
                wready = m[("wready", nch)]
                for nt in range(2):
                    isl = nt
                    iss = f"p3i{isl}"
                    for m_i in range(2):
                        gi = (nch * 2 + nt) * 2 + m_i
                        pbase = (gi % 2) * THC
                        p.wait("tensor", "p3x", xdone)
                        p.wait("tensor", wready[0], wready[1])
                        if m_i == 0:
                            p.wait("tensor", "A", m.get(("gelg", gi - 2), 0))
                        else:
                            p.wait("tensor", "V", m.get(("intg", gi - 2), 0))
                        for kt2 in range(KT):
                            for tcc in range(THC):
                                p.do("tensor", lambda e, m_i=m_i, kt2=kt2, nt=nt, tcc=tcc, pbase=pbase, wsl=wsl: e.matmul(
                                    PS[:, 512 * (pbase + tcc):512 * (pbase + tcc) + THW],
                                    p3_w[wsl][:, 4096 * m_i + 256 * kt2 + nt:4096 * m_i + 256 * (kt2 + 1):2],
                                    p3_xh[:, TH * kt2 + THW * tcc:TH * kt2 + THW * (tcc + 1)],
                                    start=(kt2 == 0), stop=(kt2 == KT - 1)),
                                    "T", 1 if (kt2 == KT - 1 and tcc == THC - 1) else 0)
                        m[("mmgrp", gi)] = p.val("T")
                        if m_i == 0:
                            m[("gbank", nt)] = pbase
                            for tcc in range(THC):
                                p.wait("scalar", "T", m[("mmgrp", gi)])
                                p.wait("scalar", "V", m.get(("intg", gi - 1), 0))
                                p.do("scalar", lambda e, tcc=tcc, pbase=pbase, nt=nt: e.activation(
                                    p3_gl[nt][:, THW * tcc:THW * (tcc + 1)],
                                    PS[:, 512 * (pbase + tcc):512 * (pbase + tcc) + THW],
                                    AF.Gelu_apprx_tanh), "A",
                                    1 if tcc == THC - 1 else 0)
                            m[("gelg", gi + 1)] = m[("gelg", gi)] = p.val("A")
                        else:
                            for tcc in range(THC):
                                p.wait("vector", "T", m[("mmgrp", gi)])
                                p.wait("vector", "A", m[("gelg", gi)])
                                if tcc == 0:
                                    p.wait("vector", iss, p.val(iss))
                                p.do("vector", lambda e, tcc=tcc, isl=isl, nt=nt, pbase=pbase: e.tensor_tensor(
                                    p3_is[isl][:, THW * tcc:THW * (tcc + 1)],
                                    p3_gl[nt][:, THW * tcc:THW * (tcc + 1)],
                                    PS[:, 512 * (pbase + tcc):512 * (pbase + tcc) + THW],
                                    ALU.mult), "V", 1)
                            m[("intg", gi + 1)] = m[("intg", gi)] = p.val("V")
                        quota = min(npieces,
                                    (npieces * (2 * nt + m_i + 1) + niter - 1) // niter)
                        while popped < quota:
                            pieces.pop(0)()
                            popped += 1
                    p.wait("scalar" if half == 0 else "sync", "V",
                           m[("intg", (nch * 2 + nt) * 2 + 1)])
                    p.do("scalar" if half == 0 else "sync",
                         lambda e, nch=nch, nt=nt, isl=isl: e.dma_start(
                             out=inter_h[128 * (2 * nch + nt):128 * (2 * nch + nt + 1),
                                         TH * half:TH * (half + 1)],
                             in_=p3_is[isl][:, 0:TH]), iss, 16)
                for f in pieces:
                    f()
                m[("mmall", nch)] = p.val("T")

            for nch in range(NCH):
                if half == 0:
                    pieces = dequant_pieces(nch)
                else:
                    wload_part(nch)
                    pieces = p4_pieces(nch)
                if nch >= 1:
                    mm_part(nch - 1, pieces)
                else:
                    for f in pieces:
                        f()
            mm_part(NCH - 1)
            p.barrier()

        def phase_p5(p: Ctx):
            m = {}
            for q in range(4):
                for oc in range(NOC):
                    og = q * NOC + oc
                    for tt in range(OTT):
                        itc = og * OTT + tt
                        xsl = itc % 16
                        xls = f"p5x{xsl}"
                        p.wait("gpsimd", "V", m.get(("cmb", itc - 16), 0))
                        p.do("gpsimd", lambda e, oc=oc, tt=tt, q=q, xsl=xsl: e.dma_start(
                            out=p5_xs[xsl][:],
                            in_=x_in[OCT * oc + 128 * tt:OCT * oc + 128 * (tt + 1),
                                     512 * q:512 * (q + 1)]), xls, 16)
                        p.do("gpsimd", lambda e, oc=oc, tt=tt, q=q, xsl=xsl: e.dma_start(
                            out=p5_ssl[xsl][:],
                            in_=s_h[OCT * oc + 128 * tt:OCT * oc + 128 * (tt + 1),
                                    512 * q:512 * (q + 1)]), xls, 16)
                        m[("xld", itc)] = p.val(xls)
                    def issue_loads(itf):
                        if itf >= 4 * NOC * NK:
                            return
                        ogf, nkf = divmod(itf, NK)
                        qf, ocf = divmod(ogf, NOC)
                        islf = itf % 6
                        gate = m.get(("mm", itf - 6), 0)
                        p.wait("sync", "T", gate)
                        p.do("sync", lambda e, nkf=nkf, ocf=ocf, islf=islf: e.dma_start(
                            out=p5_ib[islf][:],
                            in_=inter_h[128 * nkf:128 * (nkf + 1),
                                        OCT * ocf:OCT * (ocf + 1)]), f"p5i{islf}", 16)
                        m[("ib", itf)] = p.val(f"p5i{islf}")
                        p.wait("scalar", "T", gate)
                        p.do("scalar", lambda e, nkf=nkf, qf=qf, islf=islf: e.dma_start(
                            out=p5_wb[islf][:],
                            in_=wd_h[128 * nkf:128 * (nkf + 1),
                                     512 * qf:512 * (qf + 1)]), f"p5w{islf}", 16)
                        m[("wb", itf)] = p.val(f"p5w{islf}")

                    if og == 0:
                        for itf in range(6):
                            issue_loads(itf)
                    for nk in range(NK):
                        it = og * NK + nk
                        isl = it % 6
                        p.wait("tensor", f"p5i{isl}", m[("ib", it)])
                        p.wait("tensor", f"p5w{isl}", m[("wb", it)])
                        for tt in range(OTT):
                            if nk == 0:
                                p.wait("tensor", "A", m.get(("sil", (og - 1, tt)), 0))
                            p.do("tensor", lambda e, tt=tt, nk=nk, isl=isl: e.matmul(
                                PS[:, 512 * tt:512 * (tt + 1)],
                                p5_ib[isl][:, 128 * tt:128 * (tt + 1)],
                                p5_wb[isl][:],
                                start=(nk == 0), stop=(nk == NK - 1)),
                                "T", 1 if (tt == OTT - 1 or nk == NK - 1) else 0)
                            if nk == NK - 1:
                                m[("mmtt", (og, tt))] = p.val("T")
                        m[("mm", it)] = p.val("T")
                        issue_loads(it + 6)
                    for tt in range(OTT):
                        itc = og * OTT + tt
                        xsl = itc % 16
                        ows = f"p5o{itc % 4}"
                        p.wait("scalar", "T", m[("mmtt", (og, tt))])
                        p.wait("scalar", "V", m.get(("g_used", itc - 8), 0))
                        p.do("scalar", lambda e, tt=tt, itc=itc: e.activation(
                            p5_g[itc % 8][:], PS[:, 512 * tt:512 * (tt + 1)],
                            AF.Silu), "A", 1)
                        m[("sil", (og, tt))] = p.val("A")
                        p.wait("vector", "A", m[("sil", (og, tt))])
                        p.wait("vector", f"p5x{xsl}", m[("xld", itc)])
                        p.wait("vector", ows, p.val(ows))
                        p.do("vector", lambda e, itc=itc, xsl=xsl: e.tensor_tensor(
                            p5_o[itc % 4][:], p5_g[itc % 8][:], p5_ssl[xsl][:],
                            ALU.mult), "V")
                        p.do("vector", lambda e, itc=itc, xsl=xsl: e.tensor_tensor(
                            p5_o[itc % 4][:], p5_o[itc % 4][:], p5_xs[xsl][:],
                            ALU.add), "V", 1)
                        m[("cmb", itc)] = p.val("V")
                        m[("g_used", itc)] = p.val("V")
                        p.wait("sync", "V", m[("cmb", itc)])
                        p.do("sync", lambda e, oc=oc, tt=tt, q=q, itc=itc: e.dma_start(
                            out=out[OCT * oc + 128 * tt:OCT * oc + 128 * (tt + 1),
                                    512 * q:512 * (q + 1)],
                            in_=p5_o[itc % 4][:]), ows, 16)
            p.barrier()

        def build_all(pctx: Ctx):
            phase_p0_a12mh0(pctx)
            phase_a12mh1(pctx)
            phase_a3(pctx)
            phase_p3(pctx, 0)
            phase_p3(pctx, 1)
            phase_p5(pctx)

        block = st.enter_context(nc.Block())

        def reg(name):
            def f(eng):
                objs = dict(sync=nc.sync, tensor=nc.tensor, vector=nc.vector,
                            scalar=nc.scalar, gpsimd=nc.gpsimd)
                objs[name] = eng
                build_all(Ctx(name, objs, sems))
            return f

        block.sync(reg("sync"))
        block.tensor(reg("tensor"))
        block.vector(reg("vector"))
        block.scalar(reg("scalar"))
        block.gpsimd(reg("gpsimd"))

    return nc


def prep_inputs(ins):
    x = np.asarray(ins["L0_attn_x"], np.float32)
    B = x.shape[0]
    d = dict(
        wg=np.ascontiguousarray(np.asarray(ins["L0_attn_Wg"], np.float32)),
        wv=np.ascontiguousarray(np.asarray(ins["L0_attn_Wv"], np.float32)),
        wq=np.ascontiguousarray(np.asarray(ins["L0_attn_Wq"], np.float32)),
        wo=np.ascontiguousarray(np.asarray(ins["L0_attn_Wo"], np.float32)),
        bg=np.ascontiguousarray(np.asarray(ins["L0_attn_bg"], np.float32)),
        bq=np.ascontiguousarray(np.asarray(ins["L0_attn_bq"], np.float32)),
        gpkT=np.ascontiguousarray(np.asarray(ins["L0_ffn_gate_W_packed"], np.int32).T),
        upkT=np.ascontiguousarray(np.asarray(ins["L0_ffn_up_W_packed"], np.int32).T),
        gscT=np.ascontiguousarray(np.asarray(ins["L0_ffn_gate_scales"], np.float32).T),
        uscT=np.ascontiguousarray(np.asarray(ins["L0_ffn_up_scales"], np.float32).T),
        dpkT=np.ascontiguousarray(np.asarray(ins["L0_ffn_down_W_packed"], np.int32).T),
        dscT=np.ascontiguousarray(np.asarray(ins["L0_ffn_down_scales"], np.float32).T),
        identin=np.eye(128, dtype=ml_dtypes.bfloat16),
    )
    shards = [np.ascontiguousarray(x[b]) for b in range(B)]
    return d, shards


def kernel(debug=False, **inputs):
    common, shards = prep_inputs(inputs)
    S = shards[0].shape[0]
    N = common["gpkT"].shape[1]
    B = len(shards)
    nc = build_nc(S, N, debug=debug)
    in_maps = [dict(common, x=shards[b]) for b in range(B)]
    res = run_bass_kernel_spmd(nc, in_maps, core_ids=list(range(B)))
    if debug:
        return res.results
    outs = [res.results[b]["out"] for b in range(B)]
    return np.stack(outs, axis=0)


# version 16
# speedup vs baseline: 1.0973x; 1.0973x over previous
import contextlib
import numpy as np
import ml_dtypes

import concourse.bass as bass
import concourse.mybir as mybir
from concourse.bass_utils import run_bass_kernel_spmd
import concourse.bass_utils as _bu

def _enable_ldw_opt():
    f = _bu.bir_verify_and_optimise
    if any(c == "--enable-ldw-opt=false" for c in f.__code__.co_consts
           if isinstance(c, str)):
        f.__code__ = f.__code__.replace(co_consts=tuple(
            "--enable-ldw-opt=true" if c == "--enable-ldw-opt=false" else c
            for c in f.__code__.co_consts))


F32 = mybir.dt.float32
F16 = mybir.dt.float16
BF16 = mybir.dt.bfloat16
I32 = mybir.dt.int32
ALU = mybir.AluOpType
AF = mybir.ActivationFunctionType
AX = mybir.AxisListType

SCALE2 = float(np.float32(np.sqrt(1.0 - 0.95)) * np.float32(np.sqrt(1.0 - 0.95)))
DECAY = 0.95
EPS = 1e-5


class Ctx:

    def __init__(self, me, eng, sems):
        self.me = me
        self.e = eng
        self.sems = sems
        self.c = {}
        self.g = {}

    def do(self, name, f, inc=None, n=1):
        inst = f(self.e[name]) if name == self.me else None
        if inc is not None:
            self.c[inc] = self.c.get(inc, 0) + n
            if inst is not None:
                inst.then_inc(self.sems[inc], n)

    def wait(self, name, sem, val):
        if name == self.me and val > 0:
            self.e[name].wait_ge(self.sems[sem], val)

    def val(self, sem):
        return self.c.get(sem, 0)

    def barrier(self):
        for s in list(self.sems):
            v = self.c.get(s, 0)
            if v > 0:
                self.e[self.me].wait_ge(self.sems[s], v)


def build_nc(S, N, debug=False):
    E = 2048
    M = 2048
    KT = E // 128
    MT = M // 128
    MH = MT // 2
    TT = S // 128
    TC = S // 512
    NCH = N // 256
    TH = S // 2
    THW = min(512, TH)
    THC = TH // THW
    OCT = min(1024, S)
    NOC = S // OCT
    OTT = OCT // 128
    NK = N // 128

    nc = bass.Bass("TRN2", target_bir_lowering=False)

    x_in = nc.dram_tensor("x", [S, E], F32, kind="ExternalInput").ap()
    wg = nc.dram_tensor("wg", [E, M], F32, kind="ExternalInput").ap()
    wv = nc.dram_tensor("wv", [E, M], F32, kind="ExternalInput").ap()
    wq = nc.dram_tensor("wq", [E, M], F32, kind="ExternalInput").ap()
    wo = nc.dram_tensor("wo", [M, E], F32, kind="ExternalInput").ap()
    bg_in = nc.dram_tensor("bg", [M], F32, kind="ExternalInput").ap()
    bq_in = nc.dram_tensor("bq", [M], F32, kind="ExternalInput").ap()
    gpkT = nc.dram_tensor("gpkT", [E // 2, N], I32, kind="ExternalInput").ap()
    upkT = nc.dram_tensor("upkT", [E // 2, N], I32, kind="ExternalInput").ap()
    gscT = nc.dram_tensor("gscT", [E // 32, N], F32, kind="ExternalInput").ap()
    uscT = nc.dram_tensor("uscT", [E // 32, N], F32, kind="ExternalInput").ap()
    dpkT = nc.dram_tensor("dpkT", [N // 2, E], I32, kind="ExternalInput").ap()
    dscT = nc.dram_tensor("dscT", [N // 32, E], F32, kind="ExternalInput").ap()
    identin = nc.dram_tensor("identin", [128, 128], BF16, kind="ExternalInput").ap()
    out = nc.dram_tensor("out", [S, E], F32, kind="ExternalOutput").ap()

    kk = "ExternalOutput" if debug else "Internal"
    xnT_h = nc.dram_tensor("xnT_h", [E, S], BF16, kind=kk).ap()
    load_h = nc.dram_tensor("load_h", [M, S], BF16, kind=kk).ap()
    s_h = nc.dram_tensor("s_h", [S, E], BF16, kind=kk).ap()
    inter_h = nc.dram_tensor("inter_h", [N, S], BF16, kind=kk).ap()
    wgu_h = nc.dram_tensor("wgu_h", [NCH, 128, 2 * KT * 256], BF16).ap()
    wd_h = nc.dram_tensor("wd_h", [N, E], BF16, kind=kk).ap()

    with contextlib.ExitStack() as st:
        sems = {}
        slot_names = (
            ["pre", "w0", "wg1", "wv1", "wq1", "wo0", "wo1", "p3x", "T", "V", "A"]
            + [f"p0x{i}" for i in range(2)] + [f"p0st{i}" for i in range(2)]
            + [f"a1x{i}" for i in range(2)] + [f"a2l{i}" for i in range(2)]
            + [f"a3l{i}" for i in range(2)] + [f"a3s{i}" for i in range(2)]
            + [f"p3p{i}" for i in range(2)] + [f"p3s{i}" for i in range(2)]
            + [f"p3r{i}" for i in range(2)] + [f"p3i{i}" for i in range(4)]
            + [f"p4p{i}" for i in range(2)] + [f"p4w{i}" for i in range(2)]
            + [f"p5i{i}" for i in range(6)] + [f"p5w{i}" for i in range(6)]
            + [f"p5x{i}" for i in range(16)] + [f"p5o{i}" for i in range(4)]
        )
        for s_name in slot_names:
            sems[s_name] = st.enter_context(nc.semaphore(s_name))

        ident = st.enter_context(nc.sbuf_tensor("ident", [128, 128], BF16))
        bgt = st.enter_context(nc.sbuf_tensor("bgt", [128, MT], F32))
        bqt = st.enter_context(nc.sbuf_tensor("bqt", [128, MT], F32))
        dec = st.enter_context(nc.sbuf_tensor("dec", [128, 512], F32))

        PS = nc.alloc_psum_tensor("PS", [128, 4096], F32)
        arena0 = (nc._sbuf_addr_for_side("left") + 63) & ~63
        arena_end = 229344
        assert arena_end - arena0 >= 209900, f"arena too small: {arena_end - arena0}"

        sizes = {F32: 4, BF16: 2, F16: 2, I32: 4}

        def place(name, cols, dtype, off):
            sz = sizes[dtype] * cols
            assert off % 64 == 0, name
            assert arena0 + off + sz <= arena_end, f"{name}: overflow {off + sz}"
            return nc.alloc_sbuf_tensor_at(name, [128, cols], dtype,
                                           offset=arena0 + off)

        p0_x = [place(f"p0x{i}", E, F32, 8192 * i) for i in range(2)]
        p0_sq = place("p0sq", E, F32, 16384)
        p0_xn = place("p0xn", E, BF16, 24576)
        p0_st = [place(f"p0st{i}", 512, BF16, 28672 + 1024 * i) for i in range(2)]
        p0_s1 = place("p0s1", 16, F32, 30720)[:, 0:1]
        p0_s2 = place("p0s2", 16, F32, 30784)[:, 0:1]
        p0_m = place("p0m", 16, F32, 30848)[:, 0:1]
        p0_v = place("p0v", 16, F32, 30912)[:, 0:1]
        p0_r = place("p0r", 16, F32, 30976)[:, 0:1]
        p0_nm = place("p0nm", 16, F32, 31040)[:, 0:1]
        p0_mm = place("p0mm", 16, F32, 31104)[:, 0:1]

        awG = place("awG", KT * 1024, BF16, 33536)
        awV = place("awV", KT * 1024, BF16, 66304)
        awQ = place("awQ", KT * 1024, BF16, 99072)
        awG2 = place("awG2", KT * 1024, BF16, 131840)

        a12_xc = [place(f"a12xc{i}", KT * 512, BF16, 164608 + 16384 * i)
                  for i in range(2)]
        a12_st = place("a12st", 512, F32, 197376)
        a12_gt = [place(f"a12gt{i}", 512, F32, 199424 + 2048 * i) for i in range(2)]
        a12_qt = [place(f"a12qt{i}", 512, F32, 203520 + 2048 * i) for i in range(2)]
        a12_mt = place("a12mt", 512, F32, 207616)
        a12_ld = [place(f"a12ld{i}", 512, BF16, 31232 + 1024 * i) for i in range(2)]
        a12_cr = place("a12cr", MT, F32, 33280)

        wo_a = place("wo_a", MH * E, BF16, 33536)
        wo_b = place("wo_b", MH * E, BF16, 131840)
        a3_lc = [place(f"a3lc{i}", MT * 512, BF16, 164608 + 16384 * i)
                 for i in range(2)]
        a3_ss = [place(f"a3ss{i}", E, BF16, 197376 + 4096 * i) for i in range(2)]

        p3_xh = place("p3xh", KT * TH, BF16, 66304)
        p3_w = [place(f"p3w{i}", 2 * KT * 256, BF16, 16384 * i) for i in range(2)]
        p3_pk = [place("p3pk0", 2 * 8 * 256, I32, 32768),
                 place("p3pk1", 2 * 8 * 256, I32, 131840)]
        p3_t1 = place("p3t1", 256, I32, 49152)
        p3_t2 = place("p3t2", 256, I32, 50176)
        p3_t3 = place("p3t3", 256, I32, 51200)
        p3_t4 = place("p3t4", 256, I32, 52224)
        p3_lo = place("p3lo", 256, I32, 53248)
        p3_hi = place("p3hi", 256, I32, 54272)
        p3_sc = [place(f"p3sc{i}", 2 * 8 * 256, F32, 148224 + 16384 * i)
                 for i in range(2)]
        p3_gl = [place(f"p3gl{i}", THC * THW, BF16, 55296 + 4096 * i)
                 for i in range(2)]
        p3_is = [place(f"p3is{i}", TH, BF16, 180992 + sizes[BF16] * TH * i)
                 for i in range(2)]
        p4_pk = [p3_pk[0], p3_pk[1]]
        p4_sc = [p3_sc[0], p3_sc[1]]
        p4_w = [place(f"p4w{i}", E, BF16,
                      180992 + sizes[BF16] * TH * 2 + 4096 * i) for i in range(4)]

        p5_ib = [place(f"p5ib{i}", OCT, BF16, sizes[BF16] * OCT * i)
                 for i in range(6)]
        p5_wb = [place(f"p5wb{i}", 512, BF16, 12288 + 1024 * i) for i in range(6)]
        p5_g = [place(f"p5g{i}", 512, F32, 18432 + 2048 * i) for i in range(8)]
        p5_xs = [place(f"p5xs{i}", 512, F32, 34816 + 2048 * i) for i in range(16)]
        p5_ssl = [place(f"p5ssl{i}", 512, F32, 67584 + 2048 * i) for i in range(16)]
        p5_o = [place(f"p5o{i}", 512, F32, 100352 + 2048 * i) for i in range(4)]

        def load_w_half(p: Ctx, dst, srcW, mh, sem):
            for h in (0, 1):
                p.do("gpsimd", lambda e, h=h, dst=dst, srcW=srcW, mh=mh: e.dma_start(
                    out=dst[:].rearrange("p (kt c) -> p kt c", c=1024)[:, h::2, :],
                    in_=srcW[h:E:2, 1024 * mh:1024 * (mh + 1)]
                    .rearrange("(i p) c -> p i c", p=128)), sem, 16)
            return p.val(sem)

        def load_wo_half(p: Ctx, dst, a, sem):
            p.do("gpsimd", lambda e, dst=dst, a=a: e.dma_start(
                out=dst[:].rearrange("p (i c) -> p i c", c=E),
                in_=wo[1024 * a:1024 * (a + 1), :]
                .rearrange("(i p) c -> p i c", p=128)), sem, 16)
            return p.val(sem)

        def dequant_chain(p: Ctx, src, C):
            p.do("vector", lambda e: e.tensor_scalar(
                p3_t1[:, 0:C], src, 7, 9, ALU.bitwise_and, ALU.logical_shift_left), "V")
            p.do("vector", lambda e: e.tensor_scalar(
                p3_t2[:, 0:C], src, 8, 12, ALU.bitwise_and, ALU.logical_shift_left), "V")
            p.do("vector", lambda e: e.tensor_tensor(
                p3_lo[:, 0:C], p3_t1[:, 0:C], p3_t2[:, 0:C], ALU.bitwise_or), "V")
            p.do("vector", lambda e: e.tensor_scalar(
                p3_t3[:, 0:C], src, 0x70, 5, ALU.bitwise_and, ALU.logical_shift_left), "V")
            p.do("vector", lambda e: e.tensor_scalar(
                p3_t4[:, 0:C], src, 0x80, 8, ALU.bitwise_and, ALU.logical_shift_left), "V")
            p.do("vector", lambda e: e.tensor_tensor(
                p3_hi[:, 0:C], p3_t3[:, 0:C], p3_t4[:, 0:C], ALU.bitwise_or), "V")

        def ln_tile(p: Ctx, m, i, predone):
            sl = i % 2
            xs = f"p0x{sl}"
            p.wait("sync", "V", m.get(("xn", i - 2), 0))
            p.wait("sync", "A", m.get(("sq", i - 2), 0))
            p.do("sync", lambda e, i=i, sl=sl: e.dma_start(
                out=p0_x[sl][:], in_=x_in[128 * i:128 * (i + 1), :]), xs, 16)
            p.wait("scalar", xs, p.val(xs))
            p.wait("scalar", "V", m.get(("var", i - 1), 0))
            p.do("scalar", lambda e, sl=sl: e.activation(
                p0_sq[:], p0_x[sl][:], AF.Square), "A", 1)
            m[("sq", i)] = p.val("A")
            p.wait("vector", xs, p.val(xs))
            p.wait("vector", "A", m[("sq", i)])
            p.do("vector", lambda e: e.reduce_sum(
                p0_s2, p0_sq[:], axis=AX.X), "V")
            p.do("vector", lambda e, sl=sl: e.reduce_sum(
                p0_s1, p0_x[sl][:], axis=AX.X), "V", 1)
            p.wait("vector", "V", p.val("V"))
            p.do("vector", lambda e: e.tensor_scalar(
                p0_m, p0_s1, 1.0 / E, None, ALU.mult), "V")
            p.do("vector", lambda e: e.tensor_scalar(
                p0_v, p0_s2, 1.0 / E, None, ALU.mult), "V", 1)
            p.wait("vector", "V", p.val("V"))
            p.do("vector", lambda e: e.scalar_tensor_tensor(
                p0_mm, p0_m, -1.0, p0_m, ALU.mult, ALU.mult), "V", 1)
            p.wait("vector", "V", p.val("V"))
            p.do("vector", lambda e: e.tensor_tensor(
                p0_v, p0_v, p0_mm, ALU.add), "V")
            p.do("vector", lambda e: e.tensor_scalar(
                p0_v, p0_v, EPS, None, ALU.add), "V", 1)
            m[("var", i)] = p.val("V")
            p.wait("scalar", "V", m[("var", i)])
            p.do("scalar", lambda e: e.activation(p0_r, p0_v, AF.Ln), "A")
            p.do("scalar", lambda e: e.activation(
                p0_r, p0_r, AF.Exp, scale=-0.5), "A", 1)
            m[("rstd", i)] = p.val("A")
            p.wait("vector", "A", m[("rstd", i)])
            p.do("vector", lambda e: e.scalar_tensor_tensor(
                p0_nm, p0_m, -1.0, p0_r, ALU.mult, ALU.mult), "V", 1)
            p.wait("vector", "V", p.val("V"))
            p.do("vector", lambda e, sl=sl: e.tensor_scalar(
                p0_xn[:], p0_x[sl][:], p0_r, p0_nm,
                ALU.mult, ALU.add), "V", 1)
            m[("xn", i)] = p.val("V")

            for g in range(4):
                gi = 4 * i + g
                pslot = 6 + gi % 2
                sts = f"p0st{gi % 2}"
                p.wait("tensor", "pre", predone)
                p.wait("tensor", "V", m[("xn", i)])
                p.wait("tensor", "V", m.get(("ev", gi - 2), 0))
                for j in range(4):
                    kt = 4 * g + j
                    ib, h = kt // 2, kt % 2
                    p.do("tensor", lambda e, ib=ib, h=h, pslot=pslot, j=j: e.matmul(
                        PS[:, 512 * pslot + 128 * j:512 * pslot + 128 * (j + 1)],
                        p0_xn[:, 256 * ib + h:256 * (ib + 1):2],
                        ident[:], start=True, stop=True),
                        "T", 1 if j == 3 else 0)
                m[("tp", gi)] = p.val("T")
                p.wait("vector", "T", m[("tp", gi)])
                p.wait("vector", sts, p.val(sts))
                p.do("vector", lambda e, pslot=pslot, gi=gi: e.tensor_copy(
                    p0_st[gi % 2][:], PS[:, 512 * pslot:512 * (pslot + 1)]),
                    "V", 1)
                m[("ev", gi)] = p.val("V")
                p.wait("sync", "V", m[("ev", gi)])
                p.do("sync", lambda e, g=g, i=i, gi=gi: e.dma_start(
                    out=xnT_h.rearrange("(kt p) c -> p kt c", p=128)
                    [:, 4 * g:4 * (g + 1), 128 * i:128 * (i + 1)],
                    in_=p0_st[gi % 2][:].rearrange("p (j c) -> p j c", c=128)),
                    sts, 16)
            m[("stw0", i)] = p.val("p0st0")
            m[("stw1", i)] = p.val("p0st1")

        def a12_tc(p: Ctx, m, mh, tc, G, wgsem, wgdone, wvsem, wvdone,
                   wqsem, wqdone):
            sl = tc % 2
            xs = f"a1x{sl}"
            p.wait("sync", "T", m.get(("mmq", (tc - 2, MH - 1)), 0))
            p.wait("sync", "p0st0", m.get(("stw0", 4 * tc + 3), 0))
            p.wait("sync", "p0st1", m.get(("stw1", 4 * tc + 3), 0))
            p.do("sync", lambda e, tc=tc, sl=sl: e.dma_start(
                out=a12_xc[sl][:].rearrange("p (kt c) -> p kt c", c=512),
                in_=xnT_h.rearrange("(kt p) c -> p kt c", p=128)
                [:, :, 512 * tc:512 * (tc + 1)]), xs, 16)
            m[("xc", tc)] = p.val(xs)
            for mi8 in range(MH):
                mi = MH * mh + mi8
                it = tc * MH + mi8
                pg, pv, pq = (it % 2) * 3, (it % 2) * 3 + 1, (it % 2) * 3 + 2
                ls = f"a2l{it % 2}"
                p.wait("tensor", xs, m[("xc", tc)])
                p.wait("tensor", wgsem, wgdone)
                p.wait("tensor", "A", m.get(("sigg", it - 2), 0))
                for kt in range(KT):
                    p.do("tensor", lambda e, kt=kt, mi8=mi8, sl=sl, pg=pg, G=G: e.matmul(
                        PS[:, 512 * pg:512 * (pg + 1)],
                        G[:, 1024 * kt + 128 * mi8:1024 * kt + 128 * (mi8 + 1)],
                        a12_xc[sl][:, 512 * kt:512 * (kt + 1)],
                        start=(kt == 0), stop=(kt == KT - 1)),
                        "T", 1 if kt == KT - 1 else 0)
                m[("mmg", it)] = p.val("T")
                p.wait("tensor", wvsem, wvdone)
                p.wait("tensor", "V", m.get(("st", it - 2), 0))
                for kt in range(KT):
                    p.do("tensor", lambda e, kt=kt, mi8=mi8, sl=sl, pv=pv: e.matmul(
                        PS[:, 512 * pv:512 * (pv + 1)],
                        awV[:, 1024 * kt + 128 * mi8:1024 * kt + 128 * (mi8 + 1)],
                        a12_xc[sl][:, 512 * kt:512 * (kt + 1)],
                        start=(kt == 0), stop=(kt == KT - 1)),
                        "T", 1 if kt == KT - 1 else 0)
                m[("mmv", it)] = p.val("T")
                p.wait("tensor", wqsem, wqdone)
                p.wait("tensor", "A", m.get(("sigq", it - 2), 0))
                for kt in range(KT):
                    p.do("tensor", lambda e, kt=kt, mi8=mi8, sl=sl, pq=pq: e.matmul(
                        PS[:, 512 * pq:512 * (pq + 1)],
                        awQ[:, 1024 * kt + 128 * mi8:1024 * kt + 128 * (mi8 + 1)],
                        a12_xc[sl][:, 512 * kt:512 * (kt + 1)],
                        start=(kt == 0), stop=(kt == KT - 1)),
                        "T", 1 if kt == KT - 1 else 0)
                m[("mmq", (tc, mi8))] = p.val("T")
                p.wait("scalar", "T", m[("mmg", it)])
                p.wait("scalar", "V", m.get(("st", it - 2), 0))
                p.do("scalar", lambda e, mi=mi, it=it, pg=pg: e.activation(
                    a12_gt[it % 2][:], PS[:, 512 * pg:512 * (pg + 1)],
                    AF.Sigmoid, bias=bgt[:, mi:mi + 1]), "A", 1)
                m[("sigg", it)] = p.val("A")
                p.wait("scalar", "T", m[("mmq", (tc, mi8))])
                p.wait("scalar", "V", m.get(("ld", it - 2), 0))
                p.do("scalar", lambda e, mi=mi, it=it, pq=pq: e.activation(
                    a12_qt[it % 2][:], PS[:, 512 * pq:512 * (pq + 1)],
                    AF.Sigmoid, bias=bqt[:, mi:mi + 1]), "A", 1)
                m[("sigq", it)] = p.val("A")
                p.wait("vector", "T", m[("mmv", it)])
                p.wait("vector", "A", m[("sigg", it)])
                p.do("vector", lambda e, it=it, pv=pv: e.tensor_tensor(
                    a12_st[:], a12_gt[it % 2][:],
                    PS[:, 512 * pv:512 * (pv + 1)], ALU.mult), "V", 1)
                m[("st", it)] = p.val("V")
                init = 0.0 if tc == 0 else a12_cr[:, mi8:mi8 + 1]
                p.do("vector", lambda e, init=init: e.tensor_tensor_scan(
                    a12_mt[:], dec[:], a12_st[:],
                    init, ALU.mult, ALU.add), "V", 1)
                p.wait("vector", "V", p.val("V"))
                p.do("vector", lambda e, mi8=mi8: e.tensor_copy(
                    a12_cr[:, mi8:mi8 + 1], a12_mt[:, 511:512]), "V", 1)
                p.wait("vector", "A", m[("sigq", it)])
                p.wait("vector", ls, p.val(ls))
                p.do("vector", lambda e, it=it: e.scalar_tensor_tensor(
                    a12_ld[it % 2][:], a12_mt[:], SCALE2,
                    a12_qt[it % 2][:], ALU.mult, ALU.mult), "V", 1)
                m[("ld", it)] = p.val("V")
                p.wait("scalar", "V", m[("ld", it)])
                p.do("scalar", lambda e, mi=mi, tc=tc, it=it: e.dma_start(
                    out=load_h[128 * mi:128 * (mi + 1), 512 * tc:512 * (tc + 1)],
                    in_=a12_ld[it % 2][:]), ls, 16)

        def phase_p0_a12mh0(p: Ctx):
            m = {}
            g0 = load_w_half(p, awG, wg, 0, "w0")
            v0 = load_w_half(p, awV, wv, 0, "w0")
            p.g["w0"] = q0 = load_w_half(p, awQ, wq, 0, "w0")
            p.g["wg1"] = load_w_half(p, awG2, wg, 1, "wg1")
            p.do("sync", lambda e: e.dma_start(out=ident[:], in_=identin), "pre", 16)
            for mi in range(MT):
                p.do("sync", lambda e, mi=mi: e.dma_start(
                    out=bgt[:, mi:mi + 1],
                    in_=bg_in[128 * mi:128 * (mi + 1)].unsqueeze(1)), "pre", 16)
                p.do("sync", lambda e, mi=mi: e.dma_start(
                    out=bqt[:, mi:mi + 1],
                    in_=bq_in[128 * mi:128 * (mi + 1)].unsqueeze(1)), "pre", 16)
            predone = p.val("pre")
            p.do("vector", lambda e: e.memset(dec[:], DECAY), "V")
            done = 0
            for i in range(TT):
                ln_tile(p, m, i, predone)
                if i >= 7 and (i - 7) % 4 == 0:
                    a12_tc(p, m, 0, (i - 7) // 4, awG,
                           "w0", g0, "w0", v0, "w0", q0)
                    done = (i - 7) // 4 + 1
            for tc in range(done, TC):
                a12_tc(p, m, 0, tc, awG, "w0", g0, "w0", v0, "w0", q0)
            p.barrier()

        def phase_a12mh1(p: Ctx):
            m = {}
            G = awG2
            wgdone = p.g["wg1"]
            wvdone = load_w_half(p, awV, wv, 1, "wv1")
            wqdone = load_w_half(p, awQ, wq, 1, "wq1")
            p.g["wo0"] = load_wo_half(p, wo_a, 0, "wo0")
            for tc in range(TC):
                m[("stw0", 4 * tc + 3)] = 0
                m[("stw1", 4 * tc + 3)] = 0
                a12_tc(p, m, 1, tc, G, "wg1", wgdone, "wv1", wvdone,
                       "wq1", wqdone)
            p.barrier()

        def phase_a3(p: Ctx):
            m = {}
            woadone = p.g["wo0"]
            wobdone = load_wo_half(p, wo_b, 1, "wo1")
            for kt in range(KT):
                p.do("scalar", lambda e, kt=kt: e.dma_start(
                    out=p3_xh[:, TH * kt:TH * (kt + 1)],
                    in_=xnT_h[128 * kt:128 * (kt + 1), 0:TH]), "p3x", 16)
            p.g["p3x0"] = p.val("p3x")
            for tc in range(TC):
                sl = tc % 2
                lsname = f"a3l{sl}"
                p.wait("sync", "T", m.get(("mm", (tc - 2, 3)), 0))
                p.do("sync", lambda e, tc=tc, sl=sl: e.dma_start(
                    out=a3_lc[sl][:].rearrange("p (mi c) -> p mi c", c=512),
                    in_=load_h.rearrange("(mi p) c -> p mi c", p=128)
                    [:, :, 512 * tc:512 * (tc + 1)]), lsname, 16)
                m[("lc", tc)] = p.val(lsname)
                for ts in range(4):
                    it = tc * 4 + ts
                    pb = (it % 2) * 4
                    ssn = f"a3s{it % 2}"
                    p.wait("tensor", lsname, m[("lc", tc)])
                    p.wait("tensor", "wo0", woadone)
                    p.wait("tensor", "wo1", wobdone)
                    p.wait("tensor", "V", m.get(("ev", it - 2), 0))
                    for mi in range(MT):
                        for ec in range(4):
                            wob = wo_a if mi < MH else wo_b
                            p.do("tensor", lambda e, mi=mi, ec=ec, ts=ts, sl=sl, pb=pb, wob=wob: e.matmul(
                                PS[:, 512 * (pb + ec):512 * (pb + ec + 1)],
                                a3_lc[sl][:, 512 * mi + 128 * ts:512 * mi + 128 * (ts + 1)],
                                wob[:, E * (mi % MH) + 512 * ec:E * (mi % MH) + 512 * (ec + 1)],
                                start=(mi == 0), stop=(mi == MT - 1)),
                                "T", 1 if (mi == MT - 1 and ec == 3) else 0)
                    m[("mm", (tc, ts))] = p.val("T")
                    p.wait("vector", "T", m[("mm", (tc, ts))])
                    p.wait("vector", ssn, p.val(ssn))
                    p.do("vector", lambda e, it=it, pb=pb: e.tensor_copy(
                        a3_ss[it % 2][:], PS[:, 512 * pb:512 * (pb + 4)]), "V", 1)
                    m[("ev", it)] = p.val("V")
                    p.wait("scalar", "V", m[("ev", it)])
                    p.do("scalar", lambda e, tc=tc, ts=ts, it=it: e.dma_start(
                        out=s_h[512 * tc + 128 * ts:512 * tc + 128 * (ts + 1), :],
                        in_=a3_ss[it % 2][:]), ssn, 16)
            p.barrier()

        def phase_p3(p: Ctx, half):
            m = {}
            if half == 0:
                xdone = p.g["p3x0"]
            else:
                for kt in range(KT):
                    p.do("sync", lambda e, kt=kt: e.dma_start(
                        out=p3_xh[:, TH * kt:TH * (kt + 1)],
                        in_=xnT_h[128 * kt:128 * (kt + 1), TH:2 * TH]), "p3x", 16)
                xdone = p.val("p3x")

            def dequant_pieces(nch):
                wsl = nch % 2
                pks = f"p3p{wsl}"
                sps = f"p3s{wsl}"
                p.wait("sync", "V", m.get(("deq", nch - 2), 0))
                for m_i, pk in ((0, gpkT), (1, upkT)):
                    p.do("sync", lambda e, m_i=m_i, pk=pk, nch=nch, wsl=wsl: e.dma_start(
                        out=p3_pk[wsl][:, 2048 * m_i:2048 * (m_i + 1)]
                        .rearrange("p (kb c) -> p kb c", c=256),
                        in_=pk.rearrange("(kb p) c -> p kb c", p=128)
                        [:, :, 256 * nch:256 * (nch + 1)]), pks, 16)
                for m_i, sc in ((0, gscT), (1, uscT)):
                    for kb in range(8):
                        p.do("sync", lambda e, m_i=m_i, sc=sc, kb=kb, nch=nch, wsl=wsl: e.dma_start(
                            out=p3_sc[wsl][:, 2048 * m_i + 256 * kb:2048 * m_i + 256 * (kb + 1)],
                            in_=sc[8 * kb:8 * (kb + 1), 256 * nch:256 * (nch + 1)]
                            .unsqueeze(1).broadcast_to([8, 16, 256])), pks, 16)
                m[("pk", nch)] = p.val(pks)

                pieces = []

                def first_piece():
                    p.wait("vector", pks, m[("pk", nch)])
                    p.wait("vector", "T", m.get(("mmall", nch - 2), 0))
                    p.wait("vector", sps, p.val(sps))
                pieces.append(first_piece)

                def mk_piece(m_i, kb, last):
                    def piece():
                        wsl_ = nch % 2
                        coff = 2048 * m_i + 256 * kb
                        dequant_chain(p, p3_pk[wsl_][:, coff:coff + 256], 256)
                        for src_, kt2 in ((p3_lo, 2 * kb), (p3_hi, 2 * kb + 1)):
                            p.do("vector", lambda e, src_=src_, kt2=kt2, m_i=m_i, coff=coff, wsl_=wsl_: e.scalar_tensor_tensor(
                                p3_w[wsl_][:, 4096 * m_i + 256 * kt2:4096 * m_i + 256 * (kt2 + 1)],
                                src_[:].bitcast(F16)[:, 0:512:2], 16384.0,
                                p3_sc[wsl_][:, coff:coff + 256],
                                ALU.mult, ALU.mult), "V",
                                1 if last else 0)
                        if last:
                            m[("deq", nch)] = p.val("V")
                            m[("wready", nch)] = ("V", m[("deq", nch)])
                            p.wait("scalar", "V", m[("deq", nch)])
                            p.do("scalar", lambda e, nch=nch, wsl_=wsl_: e.dma_start(
                                out=wgu_h[nch], in_=p3_w[wsl_][:]),
                                f"p3s{wsl_}", 16)
                    return piece
                for m_i in range(2):
                    for kb in range(8):
                        pieces.append(mk_piece(m_i, kb, m_i == 1 and kb == 7))
                return pieces

            def wload_part(nch):
                wsl = nch % 2
                rls = f"p3r{wsl}"
                p.wait("sync", "T", m.get(("mmall", nch - 2), 0))
                p.do("sync", lambda e, nch=nch, wsl=wsl: e.dma_start(
                    out=p3_w[wsl][:], in_=wgu_h[nch]), rls, 16)
                m[("wready", nch)] = (rls, p.val(rls))

            def p4_pieces(j):
                sl = j % 2
                pks = f"p4p{sl}"
                wws = f"p4w{sl}"
                p.wait("gpsimd", "V", m.get(("p4deq", j - 2), 0))
                p.do("gpsimd", lambda e, j=j, sl=sl: e.dma_start(
                    out=p4_pk[sl][:, 0:E], in_=dpkT[128 * j:128 * (j + 1), :]), pks, 16)
                p.do("gpsimd", lambda e, j=j, sl=sl: e.dma_start(
                    out=p4_sc[sl][:, 0:E], in_=dscT[8 * j:8 * (j + 1), :]
                    .unsqueeze(1).broadcast_to([8, 16, E])), pks, 16)
                pass
                m[("p4pk", j)] = p.val(pks)

                pieces = []

                def first_piece():
                    p.wait("vector", pks, m[("p4pk", j)])
                    p.wait("vector", wws, p.val(wws))
                pieces.append(first_piece)

                def mk_piece(c, last):
                    def piece():
                        co = 256 * c
                        dequant_chain(p, p4_pk[sl][:, co:co + 256], 256)
                        for src_, wi in ((p3_lo, 0), (p3_hi, 1)):
                            p.do("vector", lambda e, src_=src_, wi=wi, sl=sl, co=co: e.scalar_tensor_tensor(
                                p4_w[2 * sl + wi][:, co:co + 256],
                                src_[:].bitcast(F16)[:, 0:512:2], 16384.0,
                                p4_sc[sl][:, co:co + 256],
                                ALU.mult, ALU.mult), "V",
                                1 if last else 0)
                        if last:
                            m[("p4deq", j)] = p.val("V")
                            p.wait("scalar", "V", m[("p4deq", j)])
                            for wi in range(2):
                                pt = 2 * j + wi
                                p.do("scalar", lambda e, pt=pt, sl=sl, wi=wi: e.dma_start(
                                    out=wd_h[128 * pt:128 * (pt + 1), :],
                                    in_=p4_w[2 * sl + wi][:]), wws, 16)
                    return piece
                for c in range(E // 256):
                    pieces.append(mk_piece(c, c == E // 256 - 1))
                return pieces

            def mm_part(nch, pieces=None):
                pieces = list(pieces) if pieces else []
                npieces = len(pieces)
                niter = 3
                popped = 0
                wsl = nch % 2
                wready = m[("wready", nch)]
                for nt in range(2):
                    isl = nt
                    iss = f"p3i{isl}"
                    for m_i in range(2):
                        gi = (nch * 2 + nt) * 2 + m_i
                        pbase = (gi % 2) * THC
                        p.wait("tensor", "p3x", xdone)
                        p.wait("tensor", wready[0], wready[1])
                        if m_i == 0:
                            p.wait("tensor", "A", m.get(("gelg", gi - 2), 0))
                        else:
                            p.wait("tensor", "V", m.get(("intg", gi - 2), 0))
                        for kt2 in range(KT):
                            for tcc in range(THC):
                                p.do("tensor", lambda e, m_i=m_i, kt2=kt2, nt=nt, tcc=tcc, pbase=pbase, wsl=wsl: e.matmul(
                                    PS[:, 512 * (pbase + tcc):512 * (pbase + tcc) + THW],
                                    p3_w[wsl][:, 4096 * m_i + 256 * kt2 + nt:4096 * m_i + 256 * (kt2 + 1):2],
                                    p3_xh[:, TH * kt2 + THW * tcc:TH * kt2 + THW * (tcc + 1)],
                                    start=(kt2 == 0), stop=(kt2 == KT - 1)),
                                    "T", 1 if (kt2 == KT - 1 and tcc == THC - 1) else 0)
                        m[("mmgrp", gi)] = p.val("T")
                        if m_i == 0:
                            m[("gbank", nt)] = pbase
                            for tcc in range(THC):
                                p.wait("scalar", "T", m[("mmgrp", gi)])
                                p.wait("scalar", "V", m.get(("intg", gi - 1), 0))
                                p.do("scalar", lambda e, tcc=tcc, pbase=pbase, nt=nt: e.activation(
                                    p3_gl[nt][:, THW * tcc:THW * (tcc + 1)],
                                    PS[:, 512 * (pbase + tcc):512 * (pbase + tcc) + THW],
                                    AF.Gelu_apprx_tanh), "A",
                                    1 if tcc == THC - 1 else 0)
                            m[("gelg", gi + 1)] = m[("gelg", gi)] = p.val("A")
                        else:
                            for tcc in range(THC):
                                p.wait("vector", "T", m[("mmgrp", gi)])
                                p.wait("vector", "A", m[("gelg", gi)])
                                if tcc == 0:
                                    p.wait("vector", iss, p.val(iss))
                                p.do("vector", lambda e, tcc=tcc, isl=isl, nt=nt, pbase=pbase: e.tensor_tensor(
                                    p3_is[isl][:, THW * tcc:THW * (tcc + 1)],
                                    p3_gl[nt][:, THW * tcc:THW * (tcc + 1)],
                                    PS[:, 512 * (pbase + tcc):512 * (pbase + tcc) + THW],
                                    ALU.mult), "V", 1)
                            m[("intg", gi + 1)] = m[("intg", gi)] = p.val("V")
                        quota = min(npieces,
                                    (npieces * (2 * nt + m_i + 1) + niter - 1) // niter)
                        while popped < quota:
                            pieces.pop(0)()
                            popped += 1
                    p.wait("scalar" if half == 0 else "sync", "V",
                           m[("intg", (nch * 2 + nt) * 2 + 1)])
                    p.do("scalar" if half == 0 else "sync",
                         lambda e, nch=nch, nt=nt, isl=isl: e.dma_start(
                             out=inter_h[128 * (2 * nch + nt):128 * (2 * nch + nt + 1),
                                         TH * half:TH * (half + 1)],
                             in_=p3_is[isl][:, 0:TH]), iss, 16)
                for f in pieces:
                    f()
                m[("mmall", nch)] = p.val("T")

            for nch in range(NCH):
                if half == 0:
                    pieces = dequant_pieces(nch)
                else:
                    wload_part(nch)
                    pieces = p4_pieces(nch)
                if nch >= 1:
                    mm_part(nch - 1, pieces)
                else:
                    for f in pieces:
                        f()
            mm_part(NCH - 1)
            p.barrier()

        def phase_p5(p: Ctx):
            m = {}
            for q in range(4):
                for oc in range(NOC):
                    og = q * NOC + oc
                    for tt in range(OTT):
                        itc = og * OTT + tt
                        xsl = itc % 16
                        xls = f"p5x{xsl}"
                        p.wait("gpsimd", "V", m.get(("cmb", itc - 16), 0))
                        p.do("gpsimd", lambda e, oc=oc, tt=tt, q=q, xsl=xsl: e.dma_start(
                            out=p5_xs[xsl][:],
                            in_=x_in[OCT * oc + 128 * tt:OCT * oc + 128 * (tt + 1),
                                     512 * q:512 * (q + 1)]), xls, 16)
                        p.do("gpsimd", lambda e, oc=oc, tt=tt, q=q, xsl=xsl: e.dma_start(
                            out=p5_ssl[xsl][:],
                            in_=s_h[OCT * oc + 128 * tt:OCT * oc + 128 * (tt + 1),
                                    512 * q:512 * (q + 1)]), xls, 16)
                        m[("xld", itc)] = p.val(xls)
                    def issue_loads(itf):
                        if itf >= 4 * NOC * NK:
                            return
                        ogf, nkf = divmod(itf, NK)
                        qf, ocf = divmod(ogf, NOC)
                        islf = itf % 6
                        gate = m.get(("mm", itf - 6), 0)
                        p.wait("sync", "T", gate)
                        p.do("sync", lambda e, nkf=nkf, ocf=ocf, islf=islf: e.dma_start(
                            out=p5_ib[islf][:],
                            in_=inter_h[128 * nkf:128 * (nkf + 1),
                                        OCT * ocf:OCT * (ocf + 1)]), f"p5i{islf}", 16)
                        m[("ib", itf)] = p.val(f"p5i{islf}")
                        p.wait("scalar", "T", gate)
                        p.do("scalar", lambda e, nkf=nkf, qf=qf, islf=islf: e.dma_start(
                            out=p5_wb[islf][:],
                            in_=wd_h[128 * nkf:128 * (nkf + 1),
                                     512 * qf:512 * (qf + 1)]), f"p5w{islf}", 16)
                        m[("wb", itf)] = p.val(f"p5w{islf}")

                    if og == 0:
                        for itf in range(6):
                            issue_loads(itf)
                    for nk in range(NK):
                        it = og * NK + nk
                        isl = it % 6
                        p.wait("tensor", f"p5i{isl}", m[("ib", it)])
                        p.wait("tensor", f"p5w{isl}", m[("wb", it)])
                        for tt in range(OTT):
                            if nk == 0:
                                p.wait("tensor", "A", m.get(("sil", (og - 1, tt)), 0))
                            p.do("tensor", lambda e, tt=tt, nk=nk, isl=isl: e.matmul(
                                PS[:, 512 * tt:512 * (tt + 1)],
                                p5_ib[isl][:, 128 * tt:128 * (tt + 1)],
                                p5_wb[isl][:],
                                start=(nk == 0), stop=(nk == NK - 1)),
                                "T", 1 if (tt == OTT - 1 or nk == NK - 1) else 0)
                            if nk == NK - 1:
                                m[("mmtt", (og, tt))] = p.val("T")
                        m[("mm", it)] = p.val("T")
                        issue_loads(it + 6)
                    for tt in range(OTT):
                        itc = og * OTT + tt
                        xsl = itc % 16
                        ows = f"p5o{itc % 4}"
                        p.wait("scalar", "T", m[("mmtt", (og, tt))])
                        p.wait("scalar", "V", m.get(("g_used", itc - 8), 0))
                        p.do("scalar", lambda e, tt=tt, itc=itc: e.activation(
                            p5_g[itc % 8][:], PS[:, 512 * tt:512 * (tt + 1)],
                            AF.Silu), "A", 1)
                        m[("sil", (og, tt))] = p.val("A")
                        p.wait("vector", "A", m[("sil", (og, tt))])
                        p.wait("vector", f"p5x{xsl}", m[("xld", itc)])
                        p.wait("vector", ows, p.val(ows))
                        p.do("vector", lambda e, itc=itc, xsl=xsl: e.tensor_tensor(
                            p5_o[itc % 4][:], p5_g[itc % 8][:], p5_ssl[xsl][:],
                            ALU.mult), "V")
                        p.do("vector", lambda e, itc=itc, xsl=xsl: e.tensor_tensor(
                            p5_o[itc % 4][:], p5_o[itc % 4][:], p5_xs[xsl][:],
                            ALU.add), "V", 1)
                        m[("cmb", itc)] = p.val("V")
                        m[("g_used", itc)] = p.val("V")
                        p.wait("sync", "V", m[("cmb", itc)])
                        p.do("sync", lambda e, oc=oc, tt=tt, q=q, itc=itc: e.dma_start(
                            out=out[OCT * oc + 128 * tt:OCT * oc + 128 * (tt + 1),
                                    512 * q:512 * (q + 1)],
                            in_=p5_o[itc % 4][:]), ows, 16)
            p.barrier()

        def build_all(pctx: Ctx):
            phase_p0_a12mh0(pctx)
            phase_a12mh1(pctx)
            phase_a3(pctx)
            phase_p3(pctx, 0)
            phase_p3(pctx, 1)
            phase_p5(pctx)

        block = st.enter_context(nc.Block())

        def reg(name):
            def f(eng):
                objs = dict(sync=nc.sync, tensor=nc.tensor, vector=nc.vector,
                            scalar=nc.scalar, gpsimd=nc.gpsimd)
                objs[name] = eng
                build_all(Ctx(name, objs, sems))
            return f

        block.sync(reg("sync"))
        block.tensor(reg("tensor"))
        block.vector(reg("vector"))
        block.scalar(reg("scalar"))
        block.gpsimd(reg("gpsimd"))

    return nc


def prep_inputs(ins):
    x = np.asarray(ins["L0_attn_x"], np.float32)
    B = x.shape[0]
    d = dict(
        wg=np.ascontiguousarray(np.asarray(ins["L0_attn_Wg"], np.float32)),
        wv=np.ascontiguousarray(np.asarray(ins["L0_attn_Wv"], np.float32)),
        wq=np.ascontiguousarray(np.asarray(ins["L0_attn_Wq"], np.float32)),
        wo=np.ascontiguousarray(np.asarray(ins["L0_attn_Wo"], np.float32)),
        bg=np.ascontiguousarray(np.asarray(ins["L0_attn_bg"], np.float32)),
        bq=np.ascontiguousarray(np.asarray(ins["L0_attn_bq"], np.float32)),
        gpkT=np.ascontiguousarray(np.asarray(ins["L0_ffn_gate_W_packed"], np.int32).T),
        upkT=np.ascontiguousarray(np.asarray(ins["L0_ffn_up_W_packed"], np.int32).T),
        gscT=np.ascontiguousarray(np.asarray(ins["L0_ffn_gate_scales"], np.float32).T),
        uscT=np.ascontiguousarray(np.asarray(ins["L0_ffn_up_scales"], np.float32).T),
        dpkT=np.ascontiguousarray(np.asarray(ins["L0_ffn_down_W_packed"], np.int32).T),
        dscT=np.ascontiguousarray(np.asarray(ins["L0_ffn_down_scales"], np.float32).T),
        identin=np.eye(128, dtype=ml_dtypes.bfloat16),
    )
    shards = [np.ascontiguousarray(x[b]) for b in range(B)]
    return d, shards


def kernel(debug=False, **inputs):
    common, shards = prep_inputs(inputs)
    S = shards[0].shape[0]
    N = common["gpkT"].shape[1]
    B = len(shards)
    nc = build_nc(S, N, debug=debug)
    in_maps = [dict(common, x=shards[b]) for b in range(B)]
    res = run_bass_kernel_spmd(nc, in_maps, core_ids=list(range(B)))
    if debug:
        return res.results
    outs = [res.results[b]["out"] for b in range(B)]
    return np.stack(outs, axis=0)


# version 17
# speedup vs baseline: 1.1406x; 1.0394x over previous
import contextlib
import numpy as np
import ml_dtypes

import concourse.bass as bass
import concourse.mybir as mybir
from concourse.bass_utils import run_bass_kernel_spmd
import concourse.bass_utils as _bu

def _enable_ldw_opt():
    f = _bu.bir_verify_and_optimise
    if any(c == "--enable-ldw-opt=false" for c in f.__code__.co_consts
           if isinstance(c, str)):
        f.__code__ = f.__code__.replace(co_consts=tuple(
            "--enable-ldw-opt=true" if c == "--enable-ldw-opt=false" else c
            for c in f.__code__.co_consts))


F32 = mybir.dt.float32
F16 = mybir.dt.float16
BF16 = mybir.dt.bfloat16
I32 = mybir.dt.int32
ALU = mybir.AluOpType
AF = mybir.ActivationFunctionType
AX = mybir.AxisListType

SCALE2 = float(np.float32(np.sqrt(1.0 - 0.95)) * np.float32(np.sqrt(1.0 - 0.95)))
DECAY = 0.95
EPS = 1e-5


class Ctx:

    def __init__(self, me, eng, sems):
        self.me = me
        self.e = eng
        self.sems = sems
        self.c = {}
        self.g = {}

    def do(self, name, f, inc=None, n=1):
        inst = f(self.e[name]) if name == self.me else None
        if inc is not None:
            self.c[inc] = self.c.get(inc, 0) + n
            if inst is not None:
                inst.then_inc(self.sems[inc], n)

    def wait(self, name, sem, val):
        if name == self.me and val > 0:
            self.e[name].wait_ge(self.sems[sem], val)

    def val(self, sem):
        return self.c.get(sem, 0)

    def barrier(self):
        for s in list(self.sems):
            v = self.c.get(s, 0)
            if v > 0:
                self.e[self.me].wait_ge(self.sems[s], v)


def build_nc(S, N, debug=False):
    E = 2048
    M = 2048
    KT = E // 128
    MT = M // 128
    MH = MT // 2
    TT = S // 128
    TC = S // 512
    NCH = N // 256
    TH = S // 2
    THW = min(512, TH)
    THC = TH // THW
    OCT = min(1024, S)
    NOC = S // OCT
    OTT = OCT // 128
    NK = N // 128

    nc = bass.Bass("TRN2", target_bir_lowering=False)

    x_in = nc.dram_tensor("x", [S, E], F32, kind="ExternalInput").ap()
    wg = nc.dram_tensor("wg", [E, M], F32, kind="ExternalInput").ap()
    wv = nc.dram_tensor("wv", [E, M], F32, kind="ExternalInput").ap()
    wq = nc.dram_tensor("wq", [E, M], F32, kind="ExternalInput").ap()
    wo = nc.dram_tensor("wo", [M, E], F32, kind="ExternalInput").ap()
    bg_in = nc.dram_tensor("bg", [M], F32, kind="ExternalInput").ap()
    bq_in = nc.dram_tensor("bq", [M], F32, kind="ExternalInput").ap()
    gpkT = nc.dram_tensor("gpkT", [E // 2, N], I32, kind="ExternalInput").ap()
    upkT = nc.dram_tensor("upkT", [E // 2, N], I32, kind="ExternalInput").ap()
    gscT = nc.dram_tensor("gscT", [E // 32, N], F32, kind="ExternalInput").ap()
    uscT = nc.dram_tensor("uscT", [E // 32, N], F32, kind="ExternalInput").ap()
    dpkT = nc.dram_tensor("dpkT", [N // 2, E], I32, kind="ExternalInput").ap()
    dscT = nc.dram_tensor("dscT", [N // 32, E], F32, kind="ExternalInput").ap()
    identin = nc.dram_tensor("identin", [128, 128], BF16, kind="ExternalInput").ap()
    out = nc.dram_tensor("out", [S, E], F32, kind="ExternalOutput").ap()

    kk = "ExternalOutput" if debug else "Internal"
    xnT_h = nc.dram_tensor("xnT_h", [E, S], BF16, kind=kk).ap()
    load_h = nc.dram_tensor("load_h", [M, S], BF16, kind=kk).ap()
    s_h = nc.dram_tensor("s_h", [S, E], BF16, kind=kk).ap()
    inter_h = nc.dram_tensor("inter_h", [N, S], BF16, kind=kk).ap()
    wgu_h = nc.dram_tensor("wgu_h", [NCH, 128, 2 * KT * 256], BF16).ap()
    wd_h = nc.dram_tensor("wd_h", [N, E], BF16, kind=kk).ap()

    with contextlib.ExitStack() as st:
        sems = {}
        slot_names = (
            ["pre", "w0", "wg1", "wv1", "wq1", "wo0", "wo1", "p3x", "T", "V", "A"]
            + [f"p0x{i}" for i in range(2)] + [f"p0st{i}" for i in range(2)]
            + [f"a1x{i}" for i in range(2)] + [f"a2l{i}" for i in range(2)]
            + [f"a3l{i}" for i in range(2)] + [f"a3s{i}" for i in range(2)]
            + [f"p3p{i}" for i in range(2)] + [f"p3s{i}" for i in range(2)]
            + [f"p3r{i}" for i in range(2)] + [f"p3i{i}" for i in range(4)]
            + [f"p4p{i}" for i in range(2)] + [f"p4w{i}" for i in range(2)]
            + [f"p5i{i}" for i in range(6)] + [f"p5w{i}" for i in range(6)]
            + [f"p5x{i}" for i in range(16)] + [f"p5o{i}" for i in range(4)]
        )
        for s_name in slot_names:
            sems[s_name] = st.enter_context(nc.semaphore(s_name))

        ident = st.enter_context(nc.sbuf_tensor("ident", [128, 128], BF16))
        bgt = st.enter_context(nc.sbuf_tensor("bgt", [128, MT], F32))
        bqt = st.enter_context(nc.sbuf_tensor("bqt", [128, MT], F32))
        dec = st.enter_context(nc.sbuf_tensor("dec", [128, 512], F32))

        PS = nc.alloc_psum_tensor("PS", [128, 4096], F32)
        arena0 = (nc._sbuf_addr_for_side("left") + 63) & ~63
        arena_end = 229344
        assert arena_end - arena0 >= 209900, f"arena too small: {arena_end - arena0}"

        sizes = {F32: 4, BF16: 2, F16: 2, I32: 4}

        def place(name, cols, dtype, off):
            sz = sizes[dtype] * cols
            assert off % 64 == 0, name
            assert arena0 + off + sz <= arena_end, f"{name}: overflow {off + sz}"
            return nc.alloc_sbuf_tensor_at(name, [128, cols], dtype,
                                           offset=arena0 + off)

        p0_x = [place(f"p0x{i}", E, F32, 8192 * i) for i in range(2)]
        p0_sq = place("p0sq", E, F32, 16384)
        p0_xn = place("p0xn", E, BF16, 24576)
        p0_st = [place(f"p0st{i}", 512, BF16, 28672 + 1024 * i) for i in range(2)]
        p0_s1 = place("p0s1", 16, F32, 30720)[:, 0:1]
        p0_s2 = place("p0s2", 16, F32, 30784)[:, 0:1]
        p0_m = place("p0m", 16, F32, 30848)[:, 0:1]
        p0_v = place("p0v", 16, F32, 30912)[:, 0:1]
        p0_r = place("p0r", 16, F32, 30976)[:, 0:1]
        p0_nm = place("p0nm", 16, F32, 31040)[:, 0:1]
        p0_mm = place("p0mm", 16, F32, 31104)[:, 0:1]

        awG = place("awG", KT * 1024, BF16, 33536)
        awV = place("awV", KT * 1024, BF16, 66304)
        awQ = place("awQ", KT * 1024, BF16, 99072)
        awG2 = place("awG2", KT * 1024, BF16, 131840)

        a12_xc = [place(f"a12xc{i}", KT * 512, BF16, 164608 + 16384 * i)
                  for i in range(2)]
        a12_st = place("a12st", 512, F32, 197376)
        a12_gt = [place(f"a12gt{i}", 512, F32, 199424 + 2048 * i) for i in range(2)]
        a12_qt = [place(f"a12qt{i}", 512, F32, 203520 + 2048 * i) for i in range(2)]
        a12_mt = place("a12mt", 512, F32, 207616)
        a12_ld = [place(f"a12ld{i}", 512, BF16, 31232 + 1024 * i) for i in range(2)]
        a12_cr = place("a12cr", MT, F32, 33280)

        wo_a = place("wo_a", MH * E, BF16, 33536)
        wo_b = place("wo_b", MH * E, BF16, 131840)
        a3_lc = [place(f"a3lc{i}", MT * 512, BF16, 164608 + 16384 * i)
                 for i in range(2)]
        a3_ss = [place(f"a3ss{i}", E, BF16, 197376 + 4096 * i) for i in range(2)]

        p3_xh = place("p3xh", KT * TH, BF16, 66304)
        p3_w = [place(f"p3w{i}", 2 * KT * 256, BF16, 16384 * i) for i in range(2)]
        p3_pk = [place("p3pk0", 2 * 8 * 256, I32, 32768),
                 place("p3pk1", 2 * 8 * 256, I32, 131840)]
        p3_t1 = place("p3t1", 256, I32, 49152)
        p3_t2 = place("p3t2", 256, I32, 50176)
        p3_t3 = place("p3t3", 256, I32, 51200)
        p3_t4 = place("p3t4", 256, I32, 52224)
        p3_lo = place("p3lo", 256, I32, 53248)
        p3_hi = place("p3hi", 256, I32, 54272)
        p3_sc = [place(f"p3sc{i}", 2 * 8 * 256, F32, 148224 + 16384 * i)
                 for i in range(2)]
        p3_gl = [place(f"p3gl{i}", THC * THW, BF16, 55296 + 4096 * i)
                 for i in range(2)]
        p3_is = [place(f"p3is{i}", TH, BF16, 180992 + sizes[BF16] * TH * i)
                 for i in range(2)]
        p4_pk = [p3_pk[0], p3_pk[1]]
        p4_sc = [p3_sc[0], p3_sc[1]]
        p4_w = [place(f"p4w{i}", E, BF16,
                      180992 + sizes[BF16] * TH * 2 + 4096 * i) for i in range(4)]

        p5_ib = [place(f"p5ib{i}", OCT, BF16, sizes[BF16] * OCT * i)
                 for i in range(6)]
        p5_wb = [place(f"p5wb{i}", 512, BF16, 12288 + 1024 * i) for i in range(6)]
        p5_g = [place(f"p5g{i}", 512, F32, 18432 + 2048 * i) for i in range(8)]
        p5_xs = [place(f"p5xs{i}", 512, F32, 34816 + 2048 * i) for i in range(16)]
        p5_ssl = [place(f"p5ssl{i}", 512, F32, 67584 + 2048 * i) for i in range(16)]
        p5_o = [place(f"p5o{i}", 512, F32, 100352 + 2048 * i) for i in range(4)]

        def load_w_half(p: Ctx, dst, srcW, mh, sem):
            for h in (0, 1):
                p.do("gpsimd", lambda e, h=h, dst=dst, srcW=srcW, mh=mh: e.dma_start(
                    out=dst[:].rearrange("p (kt c) -> p kt c", c=1024)[:, h::2, :],
                    in_=srcW[h:E:2, 1024 * mh:1024 * (mh + 1)]
                    .rearrange("(i p) c -> p i c", p=128)), sem, 16)
            return p.val(sem)

        def load_wo_half(p: Ctx, dst, a, sem):
            p.do("gpsimd", lambda e, dst=dst, a=a: e.dma_start(
                out=dst[:].rearrange("p (i c) -> p i c", c=E),
                in_=wo[1024 * a:1024 * (a + 1), :]
                .rearrange("(i p) c -> p i c", p=128)), sem, 16)
            return p.val(sem)

        def dequant_chain(p: Ctx, src, C):
            p.do("vector", lambda e: e.tensor_scalar(
                p3_t1[:, 0:C], src, 7, 9, ALU.bitwise_and, ALU.logical_shift_left), "V")
            p.do("vector", lambda e: e.tensor_scalar(
                p3_t2[:, 0:C], src, 8, 12, ALU.bitwise_and, ALU.logical_shift_left), "V")
            p.do("vector", lambda e: e.tensor_tensor(
                p3_lo[:, 0:C], p3_t1[:, 0:C], p3_t2[:, 0:C], ALU.bitwise_or), "V")
            p.do("vector", lambda e: e.tensor_scalar(
                p3_t3[:, 0:C], src, 0x70, 5, ALU.bitwise_and, ALU.logical_shift_left), "V")
            p.do("vector", lambda e: e.tensor_scalar(
                p3_t4[:, 0:C], src, 0x80, 8, ALU.bitwise_and, ALU.logical_shift_left), "V")
            p.do("vector", lambda e: e.tensor_tensor(
                p3_hi[:, 0:C], p3_t3[:, 0:C], p3_t4[:, 0:C], ALU.bitwise_or), "V")

        def ln_tile_a(p: Ctx, m, i, predone):
            sl = i % 2
            xs = f"p0x{sl}"
            p.wait("sync", "V", m.get(("xn", i - 2), 0))
            p.wait("sync", "A", m.get(("sq", i - 2), 0))
            p.do("sync", lambda e, i=i, sl=sl: e.dma_start(
                out=p0_x[sl][:], in_=x_in[128 * i:128 * (i + 1), :]), xs, 16)
            p.wait("scalar", xs, p.val(xs))
            p.wait("scalar", "V", m.get(("var", i - 1), 0))
            p.do("scalar", lambda e, sl=sl: e.activation(
                p0_sq[:], p0_x[sl][:], AF.Square), "A", 1)
            m[("sq", i)] = p.val("A")
            p.wait("vector", xs, p.val(xs))
            p.wait("vector", "A", m[("sq", i)])
            p.do("vector", lambda e: e.reduce_sum(
                p0_s2, p0_sq[:], axis=AX.X), "V")
            p.do("vector", lambda e, sl=sl: e.reduce_sum(
                p0_s1, p0_x[sl][:], axis=AX.X), "V", 1)
            p.wait("vector", "V", p.val("V"))
            p.do("vector", lambda e: e.tensor_scalar(
                p0_m, p0_s1, 1.0 / E, None, ALU.mult), "V")
            p.do("vector", lambda e: e.tensor_scalar(
                p0_v, p0_s2, 1.0 / E, None, ALU.mult), "V", 1)
            p.wait("vector", "V", p.val("V"))
            p.do("vector", lambda e: e.scalar_tensor_tensor(
                p0_mm, p0_m, -1.0, p0_m, ALU.mult, ALU.mult), "V", 1)
            p.wait("vector", "V", p.val("V"))
            p.do("vector", lambda e: e.tensor_tensor(
                p0_v, p0_v, p0_mm, ALU.add), "V")
            p.do("vector", lambda e: e.tensor_scalar(
                p0_v, p0_v, EPS, None, ALU.add), "V", 1)
            m[("var", i)] = p.val("V")
            p.wait("scalar", "V", m[("var", i)])
            p.do("scalar", lambda e: e.activation(p0_r, p0_v, AF.Ln), "A")
            p.do("scalar", lambda e: e.activation(
                p0_r, p0_r, AF.Exp, scale=-0.5), "A", 1)
            m[("rstd", i)] = p.val("A")
            p.wait("vector", "A", m[("rstd", i)])
            p.do("vector", lambda e: e.scalar_tensor_tensor(
                p0_nm, p0_m, -1.0, p0_r, ALU.mult, ALU.mult), "V", 1)
            p.wait("vector", "V", p.val("V"))
            p.do("vector", lambda e, sl=sl: e.tensor_scalar(
                p0_xn[:], p0_x[sl][:], p0_r, p0_nm,
                ALU.mult, ALU.add), "V", 1)
            m[("xn", i)] = p.val("V")

        def ln_tile_b(p: Ctx, m, i, predone):
            for g in range(4):
                gi = 4 * i + g
                pslot = 6 + gi % 2
                sts = f"p0st{gi % 2}"
                p.wait("tensor", "pre", predone)
                p.wait("tensor", "V", m[("xn", i)])
                p.wait("tensor", "V", m.get(("ev", gi - 2), 0))
                for j in range(4):
                    kt = 4 * g + j
                    ib, h = kt // 2, kt % 2
                    p.do("tensor", lambda e, ib=ib, h=h, pslot=pslot, j=j: e.matmul(
                        PS[:, 512 * pslot + 128 * j:512 * pslot + 128 * (j + 1)],
                        p0_xn[:, 256 * ib + h:256 * (ib + 1):2],
                        ident[:], start=True, stop=True),
                        "T", 1 if j == 3 else 0)
                m[("tp", gi)] = p.val("T")
                p.wait("vector", "T", m[("tp", gi)])
                p.wait("vector", sts, p.val(sts))
                p.do("vector", lambda e, pslot=pslot, gi=gi: e.tensor_copy(
                    p0_st[gi % 2][:], PS[:, 512 * pslot:512 * (pslot + 1)]),
                    "V", 1)
                m[("ev", gi)] = p.val("V")
                p.wait("sync", "V", m[("ev", gi)])
                p.do("sync", lambda e, g=g, i=i, gi=gi: e.dma_start(
                    out=xnT_h.rearrange("(kt p) c -> p kt c", p=128)
                    [:, 4 * g:4 * (g + 1), 128 * i:128 * (i + 1)],
                    in_=p0_st[gi % 2][:].rearrange("p (j c) -> p j c", c=128)),
                    sts, 16)
            m[("stw0", i)] = p.val("p0st0")
            m[("stw1", i)] = p.val("p0st1")

        def a12_tc(p: Ctx, m, mh, tc, G, wgsem, wgdone, wvsem, wvdone,
                   wqsem, wqdone, pieces=None):
            pieces = list(pieces) if pieces else []
            sl = tc % 2
            xs = f"a1x{sl}"
            p.wait("sync", "T", m.get(("mmq", (tc - 2, MH - 1)), 0))
            p.wait("sync", "p0st0", m.get(("stw0", 4 * tc + 3), 0))
            p.wait("sync", "p0st1", m.get(("stw1", 4 * tc + 3), 0))
            p.do("sync", lambda e, tc=tc, sl=sl: e.dma_start(
                out=a12_xc[sl][:].rearrange("p (kt c) -> p kt c", c=512),
                in_=xnT_h.rearrange("(kt p) c -> p kt c", p=128)
                [:, :, 512 * tc:512 * (tc + 1)]), xs, 16)
            m[("xc", tc)] = p.val(xs)
            for mi8 in range(MH):
                if pieces:
                    pieces.pop(0)()
                mi = MH * mh + mi8
                it = tc * MH + mi8
                pg, pv, pq = (it % 2) * 3, (it % 2) * 3 + 1, (it % 2) * 3 + 2
                ls = f"a2l{it % 2}"
                p.wait("tensor", xs, m[("xc", tc)])
                p.wait("tensor", wgsem, wgdone)
                p.wait("tensor", "A", m.get(("sigg", it - 2), 0))
                for kt in range(KT):
                    p.do("tensor", lambda e, kt=kt, mi8=mi8, sl=sl, pg=pg, G=G: e.matmul(
                        PS[:, 512 * pg:512 * (pg + 1)],
                        G[:, 1024 * kt + 128 * mi8:1024 * kt + 128 * (mi8 + 1)],
                        a12_xc[sl][:, 512 * kt:512 * (kt + 1)],
                        start=(kt == 0), stop=(kt == KT - 1)),
                        "T", 1 if kt == KT - 1 else 0)
                m[("mmg", it)] = p.val("T")
                p.wait("tensor", wvsem, wvdone)
                p.wait("tensor", "V", m.get(("st", it - 2), 0))
                for kt in range(KT):
                    p.do("tensor", lambda e, kt=kt, mi8=mi8, sl=sl, pv=pv: e.matmul(
                        PS[:, 512 * pv:512 * (pv + 1)],
                        awV[:, 1024 * kt + 128 * mi8:1024 * kt + 128 * (mi8 + 1)],
                        a12_xc[sl][:, 512 * kt:512 * (kt + 1)],
                        start=(kt == 0), stop=(kt == KT - 1)),
                        "T", 1 if kt == KT - 1 else 0)
                m[("mmv", it)] = p.val("T")
                p.wait("tensor", wqsem, wqdone)
                p.wait("tensor", "A", m.get(("sigq", it - 2), 0))
                for kt in range(KT):
                    p.do("tensor", lambda e, kt=kt, mi8=mi8, sl=sl, pq=pq: e.matmul(
                        PS[:, 512 * pq:512 * (pq + 1)],
                        awQ[:, 1024 * kt + 128 * mi8:1024 * kt + 128 * (mi8 + 1)],
                        a12_xc[sl][:, 512 * kt:512 * (kt + 1)],
                        start=(kt == 0), stop=(kt == KT - 1)),
                        "T", 1 if kt == KT - 1 else 0)
                m[("mmq", (tc, mi8))] = p.val("T")
                p.wait("scalar", "T", m[("mmg", it)])
                p.wait("scalar", "V", m.get(("st", it - 2), 0))
                p.do("scalar", lambda e, mi=mi, it=it, pg=pg: e.activation(
                    a12_gt[it % 2][:], PS[:, 512 * pg:512 * (pg + 1)],
                    AF.Sigmoid, bias=bgt[:, mi:mi + 1]), "A", 1)
                m[("sigg", it)] = p.val("A")
                p.wait("scalar", "T", m[("mmq", (tc, mi8))])
                p.wait("scalar", "V", m.get(("ld", it - 2), 0))
                p.do("scalar", lambda e, mi=mi, it=it, pq=pq: e.activation(
                    a12_qt[it % 2][:], PS[:, 512 * pq:512 * (pq + 1)],
                    AF.Sigmoid, bias=bqt[:, mi:mi + 1]), "A", 1)
                m[("sigq", it)] = p.val("A")
                p.wait("vector", "T", m[("mmv", it)])
                p.wait("vector", "A", m[("sigg", it)])
                p.do("vector", lambda e, it=it, pv=pv: e.tensor_tensor(
                    a12_st[:], a12_gt[it % 2][:],
                    PS[:, 512 * pv:512 * (pv + 1)], ALU.mult), "V", 1)
                m[("st", it)] = p.val("V")
                init = 0.0 if tc == 0 else a12_cr[:, mi8:mi8 + 1]
                p.do("vector", lambda e, init=init: e.tensor_tensor_scan(
                    a12_mt[:], dec[:], a12_st[:],
                    init, ALU.mult, ALU.add), "V", 1)
                p.wait("vector", "V", p.val("V"))
                p.do("vector", lambda e, mi8=mi8: e.tensor_copy(
                    a12_cr[:, mi8:mi8 + 1], a12_mt[:, 511:512]), "V", 1)
                p.wait("vector", "A", m[("sigq", it)])
                p.wait("vector", ls, p.val(ls))
                p.do("vector", lambda e, it=it: e.scalar_tensor_tensor(
                    a12_ld[it % 2][:], a12_mt[:], SCALE2,
                    a12_qt[it % 2][:], ALU.mult, ALU.mult), "V", 1)
                m[("ld", it)] = p.val("V")
                p.wait("scalar", "V", m[("ld", it)])
                p.do("scalar", lambda e, mi=mi, tc=tc, it=it: e.dma_start(
                    out=load_h[128 * mi:128 * (mi + 1), 512 * tc:512 * (tc + 1)],
                    in_=a12_ld[it % 2][:]), ls, 16)
            for f in pieces:
                f()

        def phase_p0_a12mh0(p: Ctx):
            m = {}
            g0 = load_w_half(p, awG, wg, 0, "w0")
            v0 = load_w_half(p, awV, wv, 0, "w0")
            p.g["w0"] = q0 = load_w_half(p, awQ, wq, 0, "w0")
            p.g["wg1"] = load_w_half(p, awG2, wg, 1, "wg1")
            p.do("sync", lambda e: e.dma_start(out=ident[:], in_=identin), "pre", 16)
            for mi in range(MT):
                p.do("sync", lambda e, mi=mi: e.dma_start(
                    out=bgt[:, mi:mi + 1],
                    in_=bg_in[128 * mi:128 * (mi + 1)].unsqueeze(1)), "pre", 16)
                p.do("sync", lambda e, mi=mi: e.dma_start(
                    out=bqt[:, mi:mi + 1],
                    in_=bq_in[128 * mi:128 * (mi + 1)].unsqueeze(1)), "pre", 16)
            predone = p.val("pre")
            p.do("vector", lambda e: e.memset(dec[:], DECAY), "V")

            def round_pieces(r):
                ps_ = []
                for i in range(4 * r, min(4 * r + 4, TT)):
                    ps_.append(lambda i=i: ln_tile_a(p, m, i, predone))
                    ps_.append(lambda i=i: ln_tile_b(p, m, i, predone))
                return ps_

            nrounds = (TT + 3) // 4
            for f in round_pieces(0) + round_pieces(1):
                f()
            for r in range(2, nrounds + 2):
                tc = r - 2
                a12_tc(p, m, 0, tc, awG, "w0", g0, "w0", v0, "w0", q0,
                       pieces=round_pieces(r) if r < nrounds else None)
            p.barrier()

        def phase_a12mh1(p: Ctx):
            m = {}
            G = awG2
            wgdone = p.g["wg1"]
            wvdone = load_w_half(p, awV, wv, 1, "wv1")
            wqdone = load_w_half(p, awQ, wq, 1, "wq1")
            p.g["wo0"] = load_wo_half(p, wo_a, 0, "wo0")
            for tc in range(TC):
                m[("stw0", 4 * tc + 3)] = 0
                m[("stw1", 4 * tc + 3)] = 0
                a12_tc(p, m, 1, tc, G, "wg1", wgdone, "wv1", wvdone,
                       "wq1", wqdone)
            p.barrier()

        def phase_a3(p: Ctx):
            m = {}
            woadone = p.g["wo0"]
            wobdone = load_wo_half(p, wo_b, 1, "wo1")
            for kt in range(KT):
                p.do("scalar", lambda e, kt=kt: e.dma_start(
                    out=p3_xh[:, TH * kt:TH * (kt + 1)],
                    in_=xnT_h[128 * kt:128 * (kt + 1), 0:TH]), "p3x", 16)
            p.g["p3x0"] = p.val("p3x")
            for tc in range(TC):
                sl = tc % 2
                lsname = f"a3l{sl}"
                p.wait("sync", "T", m.get(("mm", (tc - 2, 3)), 0))
                p.do("sync", lambda e, tc=tc, sl=sl: e.dma_start(
                    out=a3_lc[sl][:].rearrange("p (mi c) -> p mi c", c=512),
                    in_=load_h.rearrange("(mi p) c -> p mi c", p=128)
                    [:, :, 512 * tc:512 * (tc + 1)]), lsname, 16)
                m[("lc", tc)] = p.val(lsname)
                for ts in range(4):
                    it = tc * 4 + ts
                    pb = (it % 2) * 4
                    ssn = f"a3s{it % 2}"
                    p.wait("tensor", lsname, m[("lc", tc)])
                    p.wait("tensor", "wo0", woadone)
                    p.wait("tensor", "wo1", wobdone)
                    p.wait("tensor", "V", m.get(("ev", it - 2), 0))
                    for mi in range(MT):
                        for ec in range(4):
                            wob = wo_a if mi < MH else wo_b
                            p.do("tensor", lambda e, mi=mi, ec=ec, ts=ts, sl=sl, pb=pb, wob=wob: e.matmul(
                                PS[:, 512 * (pb + ec):512 * (pb + ec + 1)],
                                a3_lc[sl][:, 512 * mi + 128 * ts:512 * mi + 128 * (ts + 1)],
                                wob[:, E * (mi % MH) + 512 * ec:E * (mi % MH) + 512 * (ec + 1)],
                                start=(mi == 0), stop=(mi == MT - 1)),
                                "T", 1 if (mi == MT - 1 and ec == 3) else 0)
                    m[("mm", (tc, ts))] = p.val("T")
                    p.wait("vector", "T", m[("mm", (tc, ts))])
                    p.wait("vector", ssn, p.val(ssn))
                    p.do("vector", lambda e, it=it, pb=pb: e.tensor_copy(
                        a3_ss[it % 2][:], PS[:, 512 * pb:512 * (pb + 4)]), "V", 1)
                    m[("ev", it)] = p.val("V")
                    p.wait("scalar", "V", m[("ev", it)])
                    p.do("scalar", lambda e, tc=tc, ts=ts, it=it: e.dma_start(
                        out=s_h[512 * tc + 128 * ts:512 * tc + 128 * (ts + 1), :],
                        in_=a3_ss[it % 2][:]), ssn, 16)
            p.barrier()

        def phase_p3(p: Ctx, half):
            m = {}
            if half == 0:
                xdone = p.g["p3x0"]
            else:
                for kt in range(KT):
                    p.do("sync", lambda e, kt=kt: e.dma_start(
                        out=p3_xh[:, TH * kt:TH * (kt + 1)],
                        in_=xnT_h[128 * kt:128 * (kt + 1), TH:2 * TH]), "p3x", 16)
                xdone = p.val("p3x")

            def dequant_pieces(nch):
                wsl = nch % 2
                pks = f"p3p{wsl}"
                sps = f"p3s{wsl}"
                p.wait("sync", "V", m.get(("deq", nch - 2), 0))
                for m_i, pk in ((0, gpkT), (1, upkT)):
                    p.do("sync", lambda e, m_i=m_i, pk=pk, nch=nch, wsl=wsl: e.dma_start(
                        out=p3_pk[wsl][:, 2048 * m_i:2048 * (m_i + 1)]
                        .rearrange("p (kb c) -> p kb c", c=256),
                        in_=pk.rearrange("(kb p) c -> p kb c", p=128)
                        [:, :, 256 * nch:256 * (nch + 1)]), pks, 16)
                for m_i, sc in ((0, gscT), (1, uscT)):
                    for kb in range(8):
                        p.do("sync", lambda e, m_i=m_i, sc=sc, kb=kb, nch=nch, wsl=wsl: e.dma_start(
                            out=p3_sc[wsl][:, 2048 * m_i + 256 * kb:2048 * m_i + 256 * (kb + 1)],
                            in_=sc[8 * kb:8 * (kb + 1), 256 * nch:256 * (nch + 1)]
                            .unsqueeze(1).broadcast_to([8, 16, 256])), pks, 16)
                m[("pk", nch)] = p.val(pks)

                pieces = []

                def first_piece():
                    p.wait("vector", pks, m[("pk", nch)])
                    p.wait("vector", "T", m.get(("mmall", nch - 2), 0))
                    p.wait("vector", sps, p.val(sps))
                pieces.append(first_piece)

                def mk_piece(m_i, kb, last):
                    def piece():
                        wsl_ = nch % 2
                        coff = 2048 * m_i + 256 * kb
                        dequant_chain(p, p3_pk[wsl_][:, coff:coff + 256], 256)
                        for src_, kt2 in ((p3_lo, 2 * kb), (p3_hi, 2 * kb + 1)):
                            p.do("vector", lambda e, src_=src_, kt2=kt2, m_i=m_i, coff=coff, wsl_=wsl_: e.scalar_tensor_tensor(
                                p3_w[wsl_][:, 4096 * m_i + 256 * kt2:4096 * m_i + 256 * (kt2 + 1)],
                                src_[:].bitcast(F16)[:, 0:512:2], 16384.0,
                                p3_sc[wsl_][:, coff:coff + 256],
                                ALU.mult, ALU.mult), "V",
                                1 if last else 0)
                        if last:
                            m[("deq", nch)] = p.val("V")
                            m[("wready", nch)] = ("V", m[("deq", nch)])
                            p.wait("scalar", "V", m[("deq", nch)])
                            p.do("scalar", lambda e, nch=nch, wsl_=wsl_: e.dma_start(
                                out=wgu_h[nch], in_=p3_w[wsl_][:]),
                                f"p3s{wsl_}", 16)
                    return piece
                for m_i in range(2):
                    for kb in range(8):
                        pieces.append(mk_piece(m_i, kb, m_i == 1 and kb == 7))
                return pieces

            def wload_part(nch):
                wsl = nch % 2
                rls = f"p3r{wsl}"
                p.wait("sync", "T", m.get(("mmall", nch - 2), 0))
                p.do("sync", lambda e, nch=nch, wsl=wsl: e.dma_start(
                    out=p3_w[wsl][:], in_=wgu_h[nch]), rls, 16)
                m[("wready", nch)] = (rls, p.val(rls))

            def p4_pieces(j):
                sl = j % 2
                pks = f"p4p{sl}"
                wws = f"p4w{sl}"
                p.wait("gpsimd", "V", m.get(("p4deq", j - 2), 0))
                p.do("gpsimd", lambda e, j=j, sl=sl: e.dma_start(
                    out=p4_pk[sl][:, 0:E], in_=dpkT[128 * j:128 * (j + 1), :]), pks, 16)
                p.do("gpsimd", lambda e, j=j, sl=sl: e.dma_start(
                    out=p4_sc[sl][:, 0:E], in_=dscT[8 * j:8 * (j + 1), :]
                    .unsqueeze(1).broadcast_to([8, 16, E])), pks, 16)
                pass
                m[("p4pk", j)] = p.val(pks)

                pieces = []

                def first_piece():
                    p.wait("vector", pks, m[("p4pk", j)])
                    p.wait("vector", wws, p.val(wws))
                pieces.append(first_piece)

                def mk_piece(c, last):
                    def piece():
                        co = 256 * c
                        dequant_chain(p, p4_pk[sl][:, co:co + 256], 256)
                        for src_, wi in ((p3_lo, 0), (p3_hi, 1)):
                            p.do("vector", lambda e, src_=src_, wi=wi, sl=sl, co=co: e.scalar_tensor_tensor(
                                p4_w[2 * sl + wi][:, co:co + 256],
                                src_[:].bitcast(F16)[:, 0:512:2], 16384.0,
                                p4_sc[sl][:, co:co + 256],
                                ALU.mult, ALU.mult), "V",
                                1 if last else 0)
                        if last:
                            m[("p4deq", j)] = p.val("V")
                            p.wait("scalar", "V", m[("p4deq", j)])
                            for wi in range(2):
                                pt = 2 * j + wi
                                p.do("scalar", lambda e, pt=pt, sl=sl, wi=wi: e.dma_start(
                                    out=wd_h[128 * pt:128 * (pt + 1), :],
                                    in_=p4_w[2 * sl + wi][:]), wws, 16)
                    return piece
                for c in range(E // 256):
                    pieces.append(mk_piece(c, c == E // 256 - 1))
                return pieces

            def mm_part(nch, pieces=None):
                pieces = list(pieces) if pieces else []
                npieces = len(pieces)
                niter = 3
                popped = 0
                wsl = nch % 2
                wready = m[("wready", nch)]
                for nt in range(2):
                    isl = nt
                    iss = f"p3i{isl}"
                    for m_i in range(2):
                        gi = (nch * 2 + nt) * 2 + m_i
                        pbase = (gi % 2) * THC
                        p.wait("tensor", "p3x", xdone)
                        p.wait("tensor", wready[0], wready[1])
                        if m_i == 0:
                            p.wait("tensor", "A", m.get(("gelg", gi - 2), 0))
                        else:
                            p.wait("tensor", "V", m.get(("intg", gi - 2), 0))
                        for kt2 in range(KT):
                            for tcc in range(THC):
                                p.do("tensor", lambda e, m_i=m_i, kt2=kt2, nt=nt, tcc=tcc, pbase=pbase, wsl=wsl: e.matmul(
                                    PS[:, 512 * (pbase + tcc):512 * (pbase + tcc) + THW],
                                    p3_w[wsl][:, 4096 * m_i + 256 * kt2 + nt:4096 * m_i + 256 * (kt2 + 1):2],
                                    p3_xh[:, TH * kt2 + THW * tcc:TH * kt2 + THW * (tcc + 1)],
                                    start=(kt2 == 0), stop=(kt2 == KT - 1)),
                                    "T", 1 if (kt2 == KT - 1 and tcc == THC - 1) else 0)
                        m[("mmgrp", gi)] = p.val("T")
                        if m_i == 0:
                            m[("gbank", nt)] = pbase
                            for tcc in range(THC):
                                p.wait("scalar", "T", m[("mmgrp", gi)])
                                p.wait("scalar", "V", m.get(("intg", gi - 1), 0))
                                p.do("scalar", lambda e, tcc=tcc, pbase=pbase, nt=nt: e.activation(
                                    p3_gl[nt][:, THW * tcc:THW * (tcc + 1)],
                                    PS[:, 512 * (pbase + tcc):512 * (pbase + tcc) + THW],
                                    AF.Gelu_apprx_tanh), "A",
                                    1 if tcc == THC - 1 else 0)
                            m[("gelg", gi + 1)] = m[("gelg", gi)] = p.val("A")
                        else:
                            for tcc in range(THC):
                                p.wait("vector", "T", m[("mmgrp", gi)])
                                p.wait("vector", "A", m[("gelg", gi)])
                                if tcc == 0:
                                    p.wait("vector", iss, p.val(iss))
                                p.do("vector", lambda e, tcc=tcc, isl=isl, nt=nt, pbase=pbase: e.tensor_tensor(
                                    p3_is[isl][:, THW * tcc:THW * (tcc + 1)],
                                    p3_gl[nt][:, THW * tcc:THW * (tcc + 1)],
                                    PS[:, 512 * (pbase + tcc):512 * (pbase + tcc) + THW],
                                    ALU.mult), "V", 1)
                            m[("intg", gi + 1)] = m[("intg", gi)] = p.val("V")
                        quota = min(npieces,
                                    (npieces * (2 * nt + m_i + 1) + niter - 1) // niter)
                        while popped < quota:
                            pieces.pop(0)()
                            popped += 1
                    p.wait("scalar" if half == 0 else "sync", "V",
                           m[("intg", (nch * 2 + nt) * 2 + 1)])
                    p.do("scalar" if half == 0 else "sync",
                         lambda e, nch=nch, nt=nt, isl=isl: e.dma_start(
                             out=inter_h[128 * (2 * nch + nt):128 * (2 * nch + nt + 1),
                                         TH * half:TH * (half + 1)],
                             in_=p3_is[isl][:, 0:TH]), iss, 16)
                for f in pieces:
                    f()
                m[("mmall", nch)] = p.val("T")

            for nch in range(NCH):
                if half == 0:
                    pieces = dequant_pieces(nch)
                else:
                    wload_part(nch)
                    pieces = p4_pieces(nch)
                if nch >= 1:
                    mm_part(nch - 1, pieces)
                else:
                    for f in pieces:
                        f()
            mm_part(NCH - 1)
            p.barrier()

        def phase_p5(p: Ctx):
            m = {}
            for q in range(4):
                for oc in range(NOC):
                    og = q * NOC + oc
                    for tt in range(OTT):
                        itc = og * OTT + tt
                        xsl = itc % 16
                        xls = f"p5x{xsl}"
                        p.wait("gpsimd", "V", m.get(("cmb", itc - 16), 0))
                        p.do("gpsimd", lambda e, oc=oc, tt=tt, q=q, xsl=xsl: e.dma_start(
                            out=p5_xs[xsl][:],
                            in_=x_in[OCT * oc + 128 * tt:OCT * oc + 128 * (tt + 1),
                                     512 * q:512 * (q + 1)]), xls, 16)
                        p.do("gpsimd", lambda e, oc=oc, tt=tt, q=q, xsl=xsl: e.dma_start(
                            out=p5_ssl[xsl][:],
                            in_=s_h[OCT * oc + 128 * tt:OCT * oc + 128 * (tt + 1),
                                    512 * q:512 * (q + 1)]), xls, 16)
                        m[("xld", itc)] = p.val(xls)
                    def issue_loads(itf):
                        if itf >= 4 * NOC * NK:
                            return
                        ogf, nkf = divmod(itf, NK)
                        qf, ocf = divmod(ogf, NOC)
                        islf = itf % 6
                        gate = m.get(("mm", itf - 6), 0)
                        p.wait("sync", "T", gate)
                        p.do("sync", lambda e, nkf=nkf, ocf=ocf, islf=islf: e.dma_start(
                            out=p5_ib[islf][:],
                            in_=inter_h[128 * nkf:128 * (nkf + 1),
                                        OCT * ocf:OCT * (ocf + 1)]), f"p5i{islf}", 16)
                        m[("ib", itf)] = p.val(f"p5i{islf}")
                        p.wait("scalar", "T", gate)
                        p.do("scalar", lambda e, nkf=nkf, qf=qf, islf=islf: e.dma_start(
                            out=p5_wb[islf][:],
                            in_=wd_h[128 * nkf:128 * (nkf + 1),
                                     512 * qf:512 * (qf + 1)]), f"p5w{islf}", 16)
                        m[("wb", itf)] = p.val(f"p5w{islf}")

                    if og == 0:
                        for itf in range(6):
                            issue_loads(itf)
                    for nk in range(NK):
                        it = og * NK + nk
                        isl = it % 6
                        p.wait("tensor", f"p5i{isl}", m[("ib", it)])
                        p.wait("tensor", f"p5w{isl}", m[("wb", it)])
                        for tt in range(OTT):
                            if nk == 0:
                                p.wait("tensor", "A", m.get(("sil", (og - 1, tt)), 0))
                            p.do("tensor", lambda e, tt=tt, nk=nk, isl=isl: e.matmul(
                                PS[:, 512 * tt:512 * (tt + 1)],
                                p5_ib[isl][:, 128 * tt:128 * (tt + 1)],
                                p5_wb[isl][:],
                                start=(nk == 0), stop=(nk == NK - 1)),
                                "T", 1 if (tt == OTT - 1 or nk == NK - 1) else 0)
                            if nk == NK - 1:
                                m[("mmtt", (og, tt))] = p.val("T")
                        m[("mm", it)] = p.val("T")
                        issue_loads(it + 6)
                    for tt in range(OTT):
                        itc = og * OTT + tt
                        xsl = itc % 16
                        ows = f"p5o{itc % 4}"
                        p.wait("scalar", "T", m[("mmtt", (og, tt))])
                        p.wait("scalar", "V", m.get(("g_used", itc - 8), 0))
                        p.do("scalar", lambda e, tt=tt, itc=itc: e.activation(
                            p5_g[itc % 8][:], PS[:, 512 * tt:512 * (tt + 1)],
                            AF.Silu), "A", 1)
                        m[("sil", (og, tt))] = p.val("A")
                        p.wait("vector", "A", m[("sil", (og, tt))])
                        p.wait("vector", f"p5x{xsl}", m[("xld", itc)])
                        p.wait("vector", ows, p.val(ows))
                        p.do("vector", lambda e, itc=itc, xsl=xsl: e.tensor_tensor(
                            p5_o[itc % 4][:], p5_g[itc % 8][:], p5_ssl[xsl][:],
                            ALU.mult), "V")
                        p.do("vector", lambda e, itc=itc, xsl=xsl: e.tensor_tensor(
                            p5_o[itc % 4][:], p5_o[itc % 4][:], p5_xs[xsl][:],
                            ALU.add), "V", 1)
                        m[("cmb", itc)] = p.val("V")
                        m[("g_used", itc)] = p.val("V")
                        p.wait("sync", "V", m[("cmb", itc)])
                        p.do("sync", lambda e, oc=oc, tt=tt, q=q, itc=itc: e.dma_start(
                            out=out[OCT * oc + 128 * tt:OCT * oc + 128 * (tt + 1),
                                    512 * q:512 * (q + 1)],
                            in_=p5_o[itc % 4][:]), ows, 16)
            p.barrier()

        def build_all(pctx: Ctx):
            phase_p0_a12mh0(pctx)
            phase_a12mh1(pctx)
            phase_a3(pctx)
            phase_p3(pctx, 0)
            phase_p3(pctx, 1)
            phase_p5(pctx)

        block = st.enter_context(nc.Block())

        def reg(name):
            def f(eng):
                objs = dict(sync=nc.sync, tensor=nc.tensor, vector=nc.vector,
                            scalar=nc.scalar, gpsimd=nc.gpsimd)
                objs[name] = eng
                build_all(Ctx(name, objs, sems))
            return f

        block.sync(reg("sync"))
        block.tensor(reg("tensor"))
        block.vector(reg("vector"))
        block.scalar(reg("scalar"))
        block.gpsimd(reg("gpsimd"))

    return nc


def prep_inputs(ins):
    x = np.asarray(ins["L0_attn_x"], np.float32)
    B = x.shape[0]
    d = dict(
        wg=np.ascontiguousarray(np.asarray(ins["L0_attn_Wg"], np.float32)),
        wv=np.ascontiguousarray(np.asarray(ins["L0_attn_Wv"], np.float32)),
        wq=np.ascontiguousarray(np.asarray(ins["L0_attn_Wq"], np.float32)),
        wo=np.ascontiguousarray(np.asarray(ins["L0_attn_Wo"], np.float32)),
        bg=np.ascontiguousarray(np.asarray(ins["L0_attn_bg"], np.float32)),
        bq=np.ascontiguousarray(np.asarray(ins["L0_attn_bq"], np.float32)),
        gpkT=np.ascontiguousarray(np.asarray(ins["L0_ffn_gate_W_packed"], np.int32).T),
        upkT=np.ascontiguousarray(np.asarray(ins["L0_ffn_up_W_packed"], np.int32).T),
        gscT=np.ascontiguousarray(np.asarray(ins["L0_ffn_gate_scales"], np.float32).T),
        uscT=np.ascontiguousarray(np.asarray(ins["L0_ffn_up_scales"], np.float32).T),
        dpkT=np.ascontiguousarray(np.asarray(ins["L0_ffn_down_W_packed"], np.int32).T),
        dscT=np.ascontiguousarray(np.asarray(ins["L0_ffn_down_scales"], np.float32).T),
        identin=np.eye(128, dtype=ml_dtypes.bfloat16),
    )
    shards = [np.ascontiguousarray(x[b]) for b in range(B)]
    return d, shards


def kernel(debug=False, **inputs):
    common, shards = prep_inputs(inputs)
    S = shards[0].shape[0]
    N = common["gpkT"].shape[1]
    B = len(shards)
    nc = build_nc(S, N, debug=debug)
    in_maps = [dict(common, x=shards[b]) for b in range(B)]
    res = run_bass_kernel_spmd(nc, in_maps, core_ids=list(range(B)))
    if debug:
        return res.results
    outs = [res.results[b]["out"] for b in range(B)]
    return np.stack(outs, axis=0)
